# version 07
# baseline (speedup 1.0000x reference)
import numpy as np
from contextlib import ExitStack
import concourse.bass as bass
import concourse.mybir as mybir

F32 = mybir.dt.float32
BF16 = mybir.dt.bfloat16
I32 = mybir.dt.int32
AF = mybir.ActivationFunctionType
ALU = mybir.AluOpType
AX = mybir.AxisListType

MAGIC = 12582912.0


class Sem:
    __slots__ = ("h", "n")
    def __init__(self, nc, es, name):
        self.h = es.enter_context(nc.semaphore(name))
        self.n = 0


class Prog:
    ENGINES = ("sync", "vector", "scalar", "tensor", "gpsimd")
    CHAINED = ("vector", "scalar", "gpsimd")

    def __init__(self, nc, es):
        self.nc, self.es = nc, es
        self.ops = {e: [] for e in self.ENGINES}
        self._epoch = 0
        self.esem = {e: Sem(nc, es, f"e_{e}_0") for e in self.ENGINES}
        self.junk = Sem(nc, es, "junk")

    def op(self, eng, fn, waits=()):
        s = self.esem[eng]
        wl = [(w[0], w[1]) for w in waits if w is not None]
        if eng in self.CHAINED and s.n > 0:
            wl.append((s, s.n))
        s.n += 1
        self.ops[eng].append((fn, wl, s, 1))
        return (s, s.n)

    def dma(self, eng, out, in_, waits=()):
        s = self.esem[eng]
        wl = [(w[0], w[1]) for w in waits if w is not None]
        wl.append((self.junk, self.junk.n))
        s.n += 16
        self.ops[eng].append(
            (lambda e, o=out, i=in_: e.dma_start(out=o, in_=i), wl, s, 16))
        self.ops[eng].append((lambda e: e.nop(), [], self.junk, 1))
        return (s, s.n)

    def milestone(self, eng):
        s = self.esem[eng]
        return (s, s.n)

    def fence(self):
        ms = [(self.esem[e], self.esem[e].n) for e in self.ENGINES
              if self.esem[e].n > 0]
        for e in self.ENGINES:
            self.op(e, lambda eng: eng.nop(), waits=ms)
        self._epoch += 1
        self.esem = {e: Sem(self.nc, self.es, f"e_{e}_{self._epoch}")
                     for e in self.ENGINES}

    def emit(self, block):
        for e in self.ENGINES:
            ops = self.ops[e]
            if not ops:
                continue
            def body(eng, ops=ops):
                for fn, wl, inc, inck in ops:
                    seen = {}
                    for (s, v) in wl:
                        if v > 0 and seen.get(id(s), -1) < v:
                            seen[id(s)] = v
                            eng.wait_ge(s.h, v)
                    instr = fn(eng)
                    instr.then_inc(inc.h, inck)
            getattr(block, e)(body)


def sb(nc, es, name, shape, dtype):
    return es.enter_context(nc.sbuf_tensor(name, shape, dtype))


def psbank(nc, es, name):
    return es.enter_context(nc.psum_tensor(name, [128, 512], F32))


class Ident:
    def __init__(self, p, nc, es):
        iot = sb(nc, es, "idn_iot", [128, 128], I32)
        self.b16 = sb(nc, es, "ident_bf", [128, 128], BF16)
        self.f32 = sb(nc, es, "ident_f32", [128, 128], F32)
        w = p.op("gpsimd", lambda g: g.iota(out=iot[:], pattern=[[-1, 128]],
                                            base=0, channel_multiplier=1))
        p.op("vector", lambda v: v.tensor_scalar(out=self.b16[:], in0=iot[:],
                                                 scalar1=0, scalar2=None,
                                                 op0=ALU.is_equal), waits=[w])
        self.ones1 = sb(nc, es, "ones1", [1, 128], F32)
        p.op("vector", lambda v: v.memset(self.ones1[:], 1.0))
        self.ready = p.op("vector", lambda v: v.tensor_scalar(
            out=self.f32[:], in0=iot[:], scalar1=0, scalar2=None,
            op0=ALU.is_equal), waits=[w])


class TUnit:
    def __init__(self, p, nc, es, ident, name="tu"):
        self.p, self.ident = p, ident
        self.banks = [psbank(nc, es, f"{name}_a"), psbank(nc, es, f"{name}_b")]
        self.evs = [None, None]
        self._k = 0

    def t(self, src_ap, dst_ap, bf16=True, waits=(), rows=None, cols=None):
        p = self.p
        if rows is None:
            rows = 128
        if cols is None:
            cols = 128
        bank_i = self._k % 2
        self._k += 1
        bank = self.banks[bank_i]
        view = (bank[0:rows, 0:(cols + 1) // 2].bitcast(BF16)[:, 0:cols]
                if bf16 else bank[0:rows, 0:cols])
        idn = self.ident.b16 if bf16 else self.ident.f32
        w = [self.ident.ready] + list(waits)
        if self.evs[bank_i] is not None:
            w.append(self.evs[bank_i])
        tw = p.op("tensor", lambda t, s=src_ap, vv=view, idn=idn, n=cols:
                  t.transpose(out=vv, in_=s, identity=idn[0:n, 0:n]), waits=w)
        ev = p.op("scalar", lambda sc, d=dst_ap, vv=view: sc.copy(out=d, in_=vv),
                  waits=[tw])
        self.evs[bank_i] = ev
        return ev

    def done(self):
        return self.p.milestone("scalar")


class Consts:
    def __init__(self, p, nc, es, values):
        self.tile = sb(nc, es, "constcols", [128, len(values)], F32)
        self.idx = {}
        w = None
        for i, v in enumerate(values):
            self.idx[v] = i
            w = p.op("vector", lambda ve, i=i, v=v: ve.memset(
                self.tile[:, i:i + 1], v))
        self.ready = w

    def col(self, v):
        return self.tile[:, self.idx[v]:self.idx[v] + 1]


def bias_cols(p, nc, es, tu, name, hbm_vec, n):
    rowt = sb(nc, es, name + "_row", [n, 128], F32)
    colt = sb(nc, es, name + "_col", [128, n], F32)
    lw = p.dma("sync", rowt[:], hbm_vec.rearrange("(a b) -> a b", b=128))
    return colt, tu.t(rowt[:], colt[:], bf16=False, waits=[lw], cols=n)


def quant_tile(p, bufs, src_ap, out16_ap, scs_ap, waits=()):
    am, rs, vq, mg, qq = bufs["am"], bufs["rs"], bufs["vq"], bufs["mg"], bufs["qq"]
    x3 = src_ap.rearrange("p (b i) -> p b i", i=32)
    p.op("vector", lambda v, x3=x3: v.tensor_reduce(
        out=am[:], in_=x3, axis=AX.X, op=ALU.max, apply_absolute_value=True),
        waits=waits)
    p.op("vector", lambda v, scs_ap=scs_ap: v.tensor_scalar(
        out=scs_ap, in0=am[:], scalar1=1.0 / 7.5, scalar2=1e-30,
        op0=ALU.mult, op1=ALU.max))
    p.op("vector", lambda v, scs_ap=scs_ap: v.reciprocal(out=rs[:], in_=scs_ap))
    rsb = rs[:].unsqueeze(2).broadcast_to([128, 32, 32])
    p.op("vector", lambda v, x3=x3, rsb=rsb: v.tensor_tensor(
        out=vq[:].rearrange("p (b i) -> p b i", i=32), in0=x3, in1=rsb,
        op=ALU.mult))
    p.op("vector", lambda v: v.tensor_scalar(
        out=mg[:].bitcast(I32), in0=vq[:].bitcast(I32), scalar1=0x7f800000,
        scalar2=None, op0=ALU.bitwise_and))
    p.op("vector", lambda v: v.tensor_scalar(
        out=mg[:], in0=mg[:], scalar1=MAGIC * 0.125, scalar2=MAGIC * 0.125,
        op0=ALU.mult, op1=ALU.max))
    p.op("vector", lambda v: v.tensor_scalar(
        out=mg[:], in0=mg[:], scalar1=MAGIC * 0.5, scalar2=None, op0=ALU.min))
    p.op("vector", lambda v: v.tensor_tensor(out=qq[:], in0=vq[:], in1=mg[:],
                                             op=ALU.add))
    p.op("vector", lambda v: v.tensor_tensor(out=qq[:], in0=qq[:], in1=mg[:],
                                             op=ALU.subtract))
    scb = scs_ap.unsqueeze(2).broadcast_to([128, 32, 32])
    return p.op("vector", lambda v, out16_ap=out16_ap, scb=scb: v.tensor_tensor(
        out=out16_ap.rearrange("p (b i) -> p b i", i=32),
        in0=qq[:].rearrange("p (b i) -> p b i", i=32), in1=scb, op=ALU.mult))


def mk_quant_bufs(nc, es, pfx):
    return {
        "am": sb(nc, es, pfx + "_am", [128, 32], F32),
        "rs": sb(nc, es, pfx + "_rs", [128, 32], F32),
        "vq": sb(nc, es, pfx + "_vq", [128, 1024], F32),
        "mg": sb(nc, es, pfx + "_mg", [128, 1024], F32),
        "qq": sb(nc, es, pfx + "_qq", [128, 1024], F32),
    }



def emit_L0A(p, nc, es, x_hbm, W_hbm, b_hbm, fT_hbm, f_hbm, xd_hbm, xs_hbm,
             ident, tu, consts, n_batch=2):
    pMM = [psbank(nc, es, "l0a_mm0"), psbank(nc, es, "l0a_mm1")]

    b_col, b_rdy = bias_cols(p, nc, es, tu, "fb", b_hbm, 8)

    Wrow = [sb(nc, es, f"Wrow{i}", [128, 1024], F32) for i in range(2)]
    Wrow16 = [sb(nc, es, f"Wrow16{i}", [128, 1024], BF16) for i in range(2)]
    WT_sb = sb(nc, es, "WT", [128, 8 * 1024], BF16)
    cast_done = [None, None]
    tp_done = [None, None]
    for hc in range(8):
        wb, wb16 = Wrow[hc % 2], Wrow16[hc % 2]
        lw = p.dma("sync", wb[:], W_hbm[hc * 128:(hc + 1) * 128, :],
                   waits=[cast_done[hc % 2]])
        cw = p.op("vector", lambda v, wb=wb, wb16=wb16: v.tensor_copy(
            out=wb16[:], in_=wb[:]), waits=[lw, tp_done[hc % 2]])
        cast_done[hc % 2] = cw
        for dc in range(8):
            tu.t(wb16[:, dc * 128:(dc + 1) * 128],
                 WT_sb[:, dc * 1024 + hc * 128: dc * 1024 + (hc + 1) * 128],
                 bf16=True, waits=[cw])
        tp_done[hc % 2] = p.milestone("tensor")
    wt_rdy = tu.done()

    xt = [sb(nc, es, f"xt{i}", [128, 1024], F32) for i in range(2)]
    bn6 = sb(nc, es, "bn6", [128, 12], F32)
    mv = sb(nc, es, "mv", [128, 2], F32)
    rstd = sb(nc, es, "rstd", [128, 1], F32)
    nmr = sb(nc, es, "nmr", [128, 1], F32)
    xn = sb(nc, es, "xn", [128, 1024], F32)
    xn16 = [sb(nc, es, f"xn16_{i}", [128, 1024], BF16) for i in range(2)]
    scs = [sb(nc, es, f"scs{i}", [128, 32], F32) for i in range(2)]
    xd16 = [sb(nc, es, f"xd16_{i}", [128, 1024], BF16) for i in range(2)]
    qb = mk_quant_bufs(nc, es, "l0a")
    xnT = sb(nc, es, "xnT", [128, 8 * 512], BF16)
    fTs = sb(nc, es, "fTs", [128, 8 * 512], BF16)
    fnt = sb(nc, es, "fnt", [128, 1024], BF16)

    ln_done = {}; copy_done = {}; xnb_tp = {}; xd_dma = {}
    evac_done = {}; ft_dma = {}; chunk_mm = {}; f_dma = {}

    for bi in range(n_batch):
        for sc_i in range(8):
            prev_key = (bi, sc_i - 1) if sc_i else (bi - 1, 7)
            for ti in range(4):
                k = bi * 32 + sc_i * 4 + ti
                row0 = (sc_i * 4 + ti) * 128
                xbuf, xdb, xnb, scb_t = xt[k % 2], xd16[k % 2], xn16[k % 2], scs[k % 2]
                lw = p.dma("sync", xbuf[:], x_hbm[bi, row0:row0 + 128, :],
                           waits=[ln_done.get(k - 2)])
                p.op("vector", lambda v, xbuf=xbuf: v.bn_stats(
                    out=bn6[:, 0:6], in_=xbuf[:, 0:512]), waits=[lw])
                p.op("vector", lambda v, xbuf=xbuf: v.bn_stats(
                    out=bn6[:, 6:12], in_=xbuf[:, 512:1024]))
                agg = p.op("vector", lambda v: v.bn_aggr(out=mv[:], in_=bn6[:]))
                p.op("scalar", lambda sc: sc.activation(
                    out=rstd[:], in_=mv[:, 1:2], func=AF.Ln,
                    bias=consts.col(1e-5)), waits=[agg, consts.ready])
                ex1 = p.op("scalar", lambda sc: sc.activation(
                    out=rstd[:], in_=rstd[:], func=AF.Exp, scale=-0.5))
                p.op("vector", lambda v: v.tensor_scalar(
                    out=nmr[:], in0=mv[:, 0:1], scalar1=rstd[:], scalar2=-1.0,
                    op0=ALU.mult, op1=ALU.mult), waits=[ex1])
                lnw = p.op("vector", lambda v, xbuf=xbuf: v.tensor_scalar(
                    out=xn[:], in0=xbuf[:], scalar1=rstd[:], scalar2=nmr[:],
                    op0=ALU.mult, op1=ALU.add), waits=[copy_done.get(k - 1)])
                ln_done[k] = lnw
                qw = quant_tile(p, qb, xn[:], xdb[:], scb_t[:],
                                waits=[xd_dma.get(k - 2)])
                p.dma("sync", xd_hbm[bi, row0:row0 + 128, :], xdb[:], waits=[qw])
                xd_dma[k] = p.dma("sync", xs_hbm[bi, row0:row0 + 128, :],
                                  scb_t[:], waits=[qw])
                cpw = p.op("scalar", lambda sc, xnb=xnb: sc.copy(
                    out=xnb[:], in_=xn[:]), waits=[lnw, xnb_tp.get(k - 2)])
                copy_done[k] = cpw
                extra = [cpw]
                if ti == 0 and k > 0:
                    extra.append(chunk_mm.get(prev_key))
                for dc in range(8):
                    tu.t(xnb[:, dc * 128:(dc + 1) * 128],
                         xnT[:, dc * 512 + ti * 128: dc * 512 + (ti + 1) * 128],
                         bf16=True, waits=(extra if dc == 0 else [cpw]))
                xnb_tp[k] = p.milestone("tensor")
            tw = tu.done()
            for hc in range(8):
                pm = pMM[hc % 2]
                psum_free = evac_done.get((bi, sc_i, hc - 2))
                if psum_free is None and (bi, sc_i) != (0, 0):
                    psum_free = evac_done.get(prev_key + (6 + hc,))
                last = None
                for dc in range(8):
                    last = p.op("tensor", lambda t, hc=hc, dc=dc, pm=pm: t.matmul(
                        out=pm[:],
                        lhsT=WT_sb[:, dc * 1024 + hc * 128:dc * 1024 + (hc + 1) * 128],
                        rhs=xnT[:, dc * 512:(dc + 1) * 512],
                        start=(dc == 0), stop=(dc == 7)),
                        waits=([tw, wt_rdy, psum_free] if dc == 0 else []))
                ftd = (ft_dma.get(prev_key + (hc,))
                       if (bi, sc_i) != (0, 0) else None)
                ev = p.op("scalar", lambda sc, hc=hc, pm=pm: sc.activation(
                    out=fTs[:, hc * 512:(hc + 1) * 512], in_=pm[:],
                    func=AF.Identity, bias=b_col[:, hc:hc + 1]),
                    waits=[last, b_rdy, ftd])
                evac_done[(bi, sc_i, hc)] = ev
                ft_dma[(bi, sc_i, hc)] = p.dma(
                    "sync", fT_hbm[bi, hc * 128:(hc + 1) * 128,
                                   sc_i * 512:(sc_i + 1) * 512],
                    fTs[:, hc * 512:(hc + 1) * 512], waits=[ev])
            chunk_mm[(bi, sc_i)] = p.milestone("tensor")
            fpw = p.milestone("scalar")
            for q in range(4):
                for hc in range(8):
                    w = [fpw]
                    if q > 0 and hc == 0:
                        w.append(f_dma.get((bi, sc_i, q - 1)))
                    tu.t(fTs[:, hc * 512 + q * 128: hc * 512 + (q + 1) * 128],
                         fnt[:, hc * 128:(hc + 1) * 128], bf16=True, waits=w)
                f_dma[(bi, sc_i, q)] = p.dma(
                    "sync",
                    f_hbm[bi, sc_i * 512 + q * 128: sc_i * 512 + (q + 1) * 128, :],
                    fnt[:], waits=[tu.done()])
    return



def emit_wd_quant(p, nc, es, w_hbm, tu, wdT_sb, qb):
    wt = sb(nc, es, "q_wt", [128, 1024], F32)
    wd16 = sb(nc, es, "q_wd16", [128, 1024], BF16)
    scs = sb(nc, es, "q_scs", [128, 32], F32)
    prev_tp = None
    for oc in range(8):
        lw = p.dma("sync", wt[:], w_hbm[oc * 128:(oc + 1) * 128, :],
                   waits=[prev_tp])
        qw = quant_tile(p, qb, wt[:], wd16[:], scs[:], waits=[lw])
        for dc in range(8):
            tu.t(wd16[:, dc * 128:(dc + 1) * 128],
                 wdT_sb[:, dc * 1024 + oc * 128: dc * 1024 + (oc + 1) * 128],
                 bf16=True, waits=[qw])
        prev_tp = p.milestone("tensor")
    return tu.done()


def emit_band(p, nc, es, band_sb):
    bi32 = sb(nc, es, "band_i32", [128, 3 * 128], I32)
    w = None
    for i, off in enumerate((-128, 0, 128)):
        w = p.op("gpsimd", lambda g, i=i, off=off: g.iota(
            out=bi32[:, i * 128:(i + 1) * 128], pattern=[[1, 128]], base=off,
            channel_multiplier=-1))
    p.op("scalar", lambda sc: sc.activation(out=band_sb[:], in_=bi32[:],
                                            func=AF.Abs), waits=[w])
    p.op("vector", lambda v: v.tensor_scalar(
        out=band_sb[:], in0=band_sb[:], scalar1=10.0, scalar2=None,
        op0=ALU.min), waits=[p.milestone("scalar")])
    return p.op("vector", lambda v: v.tensor_scalar(
        out=band_sb[:], in0=band_sb[:], scalar1=-10.0, scalar2=0.01,
        op0=ALU.add, op1=ALU.mult))


def emit_L0B(p, nc, es, x_hbm, fT_hbm, f_hbm, xd_hbm, fw_hbm, fb_hbm, x1_hbm,
             ident, tu, n_batch=2, dbg=None, n_ib=32):
    pS = [psbank(nc, es, "l0b_s0"), psbank(nc, es, "l0b_s1")]
    pPV = [psbank(nc, es, "l0b_pv0"), psbank(nc, es, "l0b_pv1")]
    pF = [psbank(nc, es, "l0b_f0"), psbank(nc, es, "l0b_f1")]

    qb = mk_quant_bufs(nc, es, "l0b")
    wdT_sb = sb(nc, es, "wdT", [128, 8 * 1024], BF16)
    wd_rdy = emit_wd_quant(p, nc, es, fw_hbm, tu, wdT_sb, qb)
    band_sb = sb(nc, es, "band", [128, 3 * 128], F32)
    band_rdy = emit_band(p, nc, es, band_sb)
    brep = sb(nc, es, "brep", [128, 1024], F32)
    brep_rdy = p.dma("sync", brep[:],
                     fb_hbm.unsqueeze(0).broadcast_to([128, 1024]))

    fTsb = sb(nc, es, "fTsb", [128, 8 * 4096], BF16)
    fsb = sb(nc, es, "fsb", [128, 32 * 1024], BF16)
    P_sb = sb(nc, es, "P_sb", [128, 1024], BF16)
    PT_sb = sb(nc, es, "PT_sb", [128, 8 * 128], BF16)
    a_acc = sb(nc, es, "a_acc", [128, 1024], F32)
    a_sb = sb(nc, es, "a_sb", [128, 1024], F32)
    tm2 = sb(nc, es, "tm2", [128, 2], F32)
    M_run = sb(nc, es, "M_run", [128, 1], F32)
    M_new = sb(nc, es, "M_new", [128, 1], F32)
    negM = sb(nc, es, "negM", [128, 1], F32)
    dM = sb(nc, es, "dM", [128, 1], F32)
    alpha = sb(nc, es, "alpha", [128, 1], F32)
    zp = sb(nc, es, "zp", [128, 2], F32)
    Z_run = sb(nc, es, "Z_run", [128, 1], F32)
    zr = sb(nc, es, "zr", [128, 1], F32)
    xdt = sb(nc, es, "xdt", [128, 1024], BF16)
    xdT = sb(nc, es, "xdT", [128, 8 * 128], BF16)
    g_sb = sb(nc, es, "g_sb", [128, 1024], F32)
    xres = sb(nc, es, "xres", [128, 1024], F32)
    x1t = sb(nc, es, "x1t", [128, 1024], F32)

    for bi in range(n_batch):
        x1_store = p.milestone("sync")
        flw = None
        for dc in range(8):
            flw = p.dma("sync", fTsb[:, dc * 4096:(dc + 1) * 4096],
                        fT_hbm[bi, dc * 128:(dc + 1) * 128, :],
                        waits=[x1_store] if dc == 0 else [])
        for jt in range(32):
            flw = p.dma("sync", fsb[:, jt * 1024:(jt + 1) * 1024],
                        f_hbm[bi, jt * 128:(jt + 1) * 128, :])
        fl_rdy = flw
        prev_a = None
        prev_exp = None
        prev_sil = None
        for ib in range(n_ib):
            i0 = ib * 128
            pv_free = None
            for jsb in range(4):
                for jc in range(2):
                    j0 = jsb * 1024 + jc * 512
                    w0 = [fl_rdy, wd_rdy, band_rdy, brep_rdy, prev_exp]
                    last = None
                    for dc in range(8):
                        last = p.op("tensor", lambda t, dc=dc, jc=jc, j0=j0, i0=i0:
                                    t.matmul(
                            out=pS[jc][:],
                            lhsT=fTsb[:, dc * 4096 + i0: dc * 4096 + i0 + 128],
                            rhs=fTsb[:, dc * 4096 + j0: dc * 4096 + j0 + 512],
                            start=(dc == 0), stop=(dc == 7)),
                            waits=(w0 if dc == 0 else []))
                mmw = p.milestone("tensor")
                for q in range(8):
                    off = jsb * 1024 + q * 128 - i0
                    if off in (-128, 0, 128):
                        bs = (off + 128) // 128
                        jc, qq = divmod(q, 4)
                        p.op("vector", lambda v, jc=jc, qq=qq, bs=bs: v.tensor_tensor(
                            out=pS[jc][:, qq * 128:(qq + 1) * 128],
                            in0=pS[jc][:, qq * 128:(qq + 1) * 128],
                            in1=band_sb[:, bs * 128:(bs + 1) * 128], op=ALU.add),
                            waits=[mmw])
                p.op("vector", lambda v: v.tensor_reduce(
                    out=tm2[:, 0:1], in_=pS[0][:], axis=AX.X, op=ALU.max),
                    waits=[mmw])
                p.op("vector", lambda v: v.tensor_reduce(
                    out=tm2[:, 1:2], in_=pS[1][:], axis=AX.X, op=ALU.max))
                if jsb == 0:
                    p.op("vector", lambda v: v.tensor_scalar(
                        out=M_new[:], in0=tm2[:, 0:1], scalar1=tm2[:, 1:2],
                        scalar2=None, op0=ALU.max))
                else:
                    p.op("vector", lambda v: v.tensor_scalar(
                        out=M_new[:], in0=tm2[:, 0:1], scalar1=tm2[:, 1:2],
                        scalar2=M_run[:], op0=ALU.max, op1=ALU.max))
                    p.op("vector", lambda v: v.tensor_scalar(
                        out=dM[:], in0=M_run[:], scalar1=M_new[:], scalar2=None,
                        op0=ALU.subtract))
                p.op("vector", lambda v: v.tensor_scalar(
                    out=negM[:], in0=M_new[:], scalar1=-1.0, scalar2=None,
                    op0=ALU.mult))
                vew = p.op("vector", lambda v: v.tensor_copy(out=M_run[:],
                                                             in_=M_new[:]))
                if jsb > 0:
                    p.op("scalar", lambda sc: sc.activation(
                        out=alpha[:], in_=dM[:], func=AF.Exp), waits=[vew])
                p.op("scalar", lambda sc: sc.activation(
                    out=P_sb[:, 0:512], in_=pS[0][:], func=AF.Exp, bias=negM[:]),
                    waits=[vew])
                exw = p.op("scalar", lambda sc: sc.activation(
                    out=P_sb[:, 512:1024], in_=pS[1][:], func=AF.Exp,
                    bias=negM[:]))
                prev_exp = exw
                p.op("vector", lambda v: v.tensor_reduce(
                    out=zp[:, 0:1], in_=P_sb[:, 0:512], axis=AX.X, op=ALU.add),
                    waits=[exw])
                zpw = p.op("vector", lambda v: v.tensor_reduce(
                    out=zp[:, 1:2], in_=P_sb[:, 512:1024], axis=AX.X, op=ALU.add))
                if jsb == 0:
                    p.op("vector", lambda v: v.tensor_scalar(
                        out=Z_run[:], in0=zp[:, 0:1], scalar1=zp[:, 1:2],
                        scalar2=None, op0=ALU.add), waits=[zpw])
                else:
                    p.op("vector", lambda v: v.tensor_scalar(
                        out=Z_run[:], in0=Z_run[:], scalar1=alpha[:],
                        scalar2=zp[:, 0:1], op0=ALU.mult, op1=ALU.add),
                        waits=[zpw, p.milestone("scalar")])
                    p.op("vector", lambda v: v.tensor_scalar(
                        out=Z_run[:], in0=Z_run[:], scalar1=zp[:, 1:2],
                        scalar2=None, op0=ALU.add))
                for q in range(8):
                    tu.t(P_sb[:, q * 128:(q + 1) * 128],
                         PT_sb[:, q * 128:(q + 1) * 128], bf16=True, waits=[exw])
                ptw = tu.done()
                last = None
                for jb in range(8):
                    jt = jsb * 8 + jb
                    for oc in range(2):
                        last = p.op("tensor", lambda t, jb=jb, jt=jt, oc=oc:
                                    t.matmul(
                            out=pPV[oc][:],
                            lhsT=PT_sb[:, jb * 128:(jb + 1) * 128],
                            rhs=fsb[:, jt * 1024 + oc * 512:
                                    jt * 1024 + (oc + 1) * 512],
                            start=(jb == 0), stop=(jb == 7)),
                            waits=([ptw, pv_free] if (jb == 0 and oc == 0)
                                   else []))
                pvw = p.milestone("tensor")
                if jsb == 0:
                    p.op("scalar", lambda sc: sc.copy(out=a_acc[:, 0:512],
                                                      in_=pPV[0][:]),
                         waits=[pvw, prev_a])
                    mg2 = p.op("scalar", lambda sc: sc.copy(
                        out=a_acc[:, 512:1024], in_=pPV[1][:]))
                    pv_free = mg2
                else:
                    p.op("vector", lambda v: v.scalar_tensor_tensor(
                        out=a_acc[:, 0:512], in0=a_acc[:, 0:512],
                        scalar=alpha[:], in1=pPV[0][:], op0=ALU.mult,
                        op1=ALU.add), waits=[pvw, p.milestone("scalar")])
                    pv_free = p.op("vector", lambda v: v.scalar_tensor_tensor(
                        out=a_acc[:, 512:1024], in0=a_acc[:, 512:1024],
                        scalar=alpha[:], in1=pPV[1][:], op0=ALU.mult,
                        op1=ALU.add))
            p.op("vector", lambda v: v.reciprocal(out=zr[:], in_=Z_run[:]),
                 waits=[p.milestone("scalar")])
            aw = p.op("vector", lambda v: v.tensor_scalar(
                out=a_sb[:], in0=a_acc[:], scalar1=zr[:], scalar2=None,
                op0=ALU.mult))
            prev_a = aw
            if dbg is not None:
                p.dma("sync", dbg["a"][bi, i0:i0 + 128, :], a_sb[:], waits=[aw])
                p.dma("sync", dbg["z"][bi, i0:i0 + 128, :], Z_run[:], waits=[aw])
                p.dma("sync", dbg["m"][bi, i0:i0 + 128, :], M_run[:], waits=[aw])
            xw = p.dma("sync", xdt[:], xd_hbm[bi, i0:i0 + 128, :],
                       waits=[tu.done()])
            for dc in range(8):
                tu.t(xdt[:, dc * 128:(dc + 1) * 128],
                     xdT[:, dc * 128:(dc + 1) * 128], bf16=True, waits=[xw])
            xtw = tu.done()
            last = None
            for oc in range(2):
                for dc in range(8):
                    last = p.op("tensor", lambda t, oc=oc, dc=dc: t.matmul(
                        out=pF[oc][:],
                        lhsT=xdT[:, dc * 128:(dc + 1) * 128],
                        rhs=wdT_sb[:, dc * 1024 + oc * 512:
                                   dc * 1024 + (oc + 1) * 512],
                        start=(dc == 0), stop=(dc == 7)),
                        waits=([xtw, prev_sil] if (oc == 0 and dc == 0) else []))
            fmw = p.milestone("tensor")
            p.op("vector", lambda v: v.tensor_tensor(
                out=g_sb[:, 0:512], in0=pF[0][:], in1=brep[:, 0:512],
                op=ALU.add), waits=[fmw])
            gw = p.op("vector", lambda v: v.tensor_tensor(
                out=g_sb[:, 512:1024], in0=pF[1][:], in1=brep[:, 512:1024],
                op=ALU.add))
            sil = p.op("scalar", lambda sc: sc.activation(
                out=g_sb[:], in_=g_sb[:], func=AF.Silu), waits=[gw])
            prev_sil = sil
            rw = p.dma("sync", xres[:], x_hbm[bi, i0:i0 + 128, :])
            p.op("vector", lambda v: v.tensor_tensor(
                out=x1t[:], in0=a_sb[:], in1=g_sb[:], op=ALU.mult),
                waits=[sil, rw, aw])
            xw1 = p.op("vector", lambda v: v.tensor_tensor(
                out=x1t[:], in0=x1t[:], in1=xres[:], op=ALU.add))
            p.dma("sync", x1_hbm[bi, i0:i0 + 128, :], x1t[:], waits=[xw1])
    return



def emit_L1A(p, nc, es, x1_hbm, Alog_hbm, Wdt_hbm, bdt_hbm, Wxp_hbm, bxp_hbm,
             Wout_hbm, bout_hbm, xn1T_hbm, s_hbm, x0_hbm, bct_hbm,
             ident, tu, consts, n_batch=2, n_chunk=8, dbg=None):
    import os
    _ser = os.environ.get("SERIALIZE_SCAN", "0") == "1"
    def _allm():
        return [p.milestone(e) for e in ("sync", "vector", "scalar", "tensor")] if _ser else []
    """x0_hbm [nb, 4096] f32 Internal; bct_hbm [nb, 64, 4096] f32 Internal.
    Outputs: xn1T_hbm [nb, 1024, 4096] bf16, sT_hbm [nb, 1024, 4096] f32."""
    pBC = psbank(nc, es, "l1a_bc")
    pSP = [[psbank(nc, es, "l1a_sp0"), psbank(nc, es, "l1a_sp1")],
           [psbank(nc, es, "l1a_sp2"), psbank(nc, es, "l1a_sp3")]]

    A_sb = sb(nc, es, "A_sb", [128, 8 * 16], F32)
    aw = None
    for dg in range(8):
        aw = p.dma("sync", A_sb[:, dg * 16:(dg + 1) * 16],
                   Alog_hbm[dg * 128:(dg + 1) * 128, :])
    p.op("scalar", lambda sc: sc.activation(out=A_sb[:], in_=A_sb[:],
                                            func=AF.Exp), waits=[aw])
    A_rdy = p.op("vector", lambda v: v.tensor_scalar(
        out=A_sb[:], in0=A_sb[:], scalar1=-1.0, scalar2=None, op0=ALU.mult),
        waits=[p.milestone("scalar")])
    wdt_col, wdt_rdy = bias_cols(p, nc, es, tu, "wdt", Wdt_hbm.flatten(), 8)
    bdt_col, bdt_rdy = bias_cols(p, nc, es, tu, "bdt", bdt_hbm, 8)
    bout_row = sb(nc, es, "bout_row", [1, 1024], F32)
    bout_rep = sb(nc, es, "bout_rep", [128, 1024], F32)
    borw = p.dma("sync", bout_row[:], bout_hbm.unsqueeze(0))
    bout_rep_rdy = None
    for oh in range(2):
        bmm = p.op("tensor", lambda t, oh=oh: t.matmul(
            out=tu.banks[0][:], lhsT=ident.ones1[:],
            rhs=bout_row[:, oh * 512:(oh + 1) * 512], start=True, stop=True),
            waits=[borw, ident.ready, tu.evs[0], bout_rep_rdy])
        bout_rep_rdy = p.op("scalar", lambda sc, oh=oh: sc.copy(
            out=bout_rep[:, oh * 512:(oh + 1) * 512], in_=tu.banks[0][:]),
            waits=[bmm])
    tu.evs[0] = bout_rep_rdy
    Wxp_sb = sb(nc, es, "Wxp_s", [32, 1024], F32)
    Wxp16 = sb(nc, es, "Wxp16", [32, 1024], BF16)
    WxpT = sb(nc, es, "WxpT", [128, 8 * 64], BF16)
    tmp32 = sb(nc, es, "wxp_tmp", [128, 32], BF16)
    lw = p.dma("sync", Wxp_sb[:], Wxp_hbm)
    cwx = p.op("vector", lambda v: v.tensor_copy(out=Wxp16[:], in_=Wxp_sb[:]),
               waits=[lw])
    zwx = p.op("vector", lambda v: v.memset(WxpT[:], 0))
    wxp_rdy = None
    for dc in range(8):
        ev = tu.t(Wxp16[:, dc * 128:(dc + 1) * 128], tmp32[:], bf16=True,
                  waits=[cwx, zwx, wxp_rdy], cols=32)
        p.op("vector", lambda v, dc=dc: v.tensor_copy(
            out=WxpT[:, dc * 64:dc * 64 + 16], in_=tmp32[:, 0:16]), waits=[ev])
        wxp_rdy = p.op("vector", lambda v, dc=dc: v.tensor_copy(
            out=WxpT[:, dc * 64 + 32:dc * 64 + 48], in_=tmp32[:, 16:32]))
    bxp_pad = sb(nc, es, "bxp_pad", [64, 1], F32)
    bxz = p.op("vector", lambda v: v.memset(bxp_pad[:], 0.0))
    p.dma("sync", bxp_pad[0:16, :], bxp_hbm[0:16].unsqueeze(1), waits=[bxz])
    bxp_rdy = p.dma("sync", bxp_pad[32:48, :], bxp_hbm[16:32].unsqueeze(1))
    Worow = [sb(nc, es, f"worow{i}", [128, 1024], F32) for i in range(2)]
    Worow16 = [sb(nc, es, f"worow16{i}", [128, 1024], BF16) for i in range(2)]
    WoT = sb(nc, es, "WoT", [128, 8 * 1024], BF16)
    cast_done = [None, None]
    tp_done = [None, None]
    for oc in range(8):
        wb, wb16 = Worow[oc % 2], Worow16[oc % 2]
        lww = p.dma("sync", wb[:], Wout_hbm[oc * 128:(oc + 1) * 128, :],
                    waits=[cast_done[oc % 2]])
        cw = p.op("vector", lambda v, wb=wb, wb16=wb16: v.tensor_copy(
            out=wb16[:], in_=wb[:]), waits=[lww, tp_done[oc % 2]])
        cast_done[oc % 2] = cw
        for dc in range(8):
            tu.t(wb16[:, dc * 128:(dc + 1) * 128],
                 WoT[:, dc * 1024 + oc * 128: dc * 1024 + (oc + 1) * 128],
                 bf16=True, waits=[cw])
        tp_done[oc % 2] = p.milestone("tensor")
    wo_rdy = tu.done()

    xt = [sb(nc, es, f"l1xt{i}", [128, 1024], F32) for i in range(2)]
    bn6 = sb(nc, es, "l1bn6", [128, 12], F32)
    mv = sb(nc, es, "l1mv", [128, 2], F32)
    rstd = sb(nc, es, "l1rstd", [128, 1], F32)
    nmr = sb(nc, es, "l1nmr", [128, 1], F32)
    xn = sb(nc, es, "l1xn", [128, 1024], F32)
    xn16 = [sb(nc, es, f"l1xn16_{i}", [128, 1024], BF16) for i in range(2)]
    x0cols = sb(nc, es, "x0cols", [128, 32], F32)
    x0T = sb(nc, es, "x0T", [32, 128], F32)
    xn1T_stage = sb(nc, es, "xn1T_stage", [128, 1024], BF16)
    xsl = [sb(nc, es, f"xsl{i}", [128, 512], BF16) for i in range(2)]
    x0rep = sb(nc, es, "x0rep", [128, 4096], F32)
    BCT = sb(nc, es, "BCT", [64, 4096], F32)

    dtT = sb(nc, es, "dtT", [128, 8 * 512], F32)
    uT = sb(nc, es, "uT", [128, 8 * 512], F32)
    zt = sb(nc, es, "zt", [128, 512], F32)
    azt = sb(nc, es, "azt", [128, 512], F32)
    relz = sb(nc, es, "relz", [128, 512], F32)
    Brep = [sb(nc, es, f"Brep{i}", [128, 512], F32) for i in range(2)]
    Crep = [sb(nc, es, f"Crep{i}", [128, 512], F32) for i in range(2)]
    dA_t = [sb(nc, es, f"dA{i}", [128, 512], F32) for i in range(2)]
    W_t = sb(nc, es, "W_t", [128, 512], F32)
    state = sb(nc, es, "state", [128, 512], F32)
    tmpy = sb(nc, es, "tmpy", [128, 512], F32)
    carry = sb(nc, es, "carry", [128, 8 * 16], F32)
    yT = sb(nc, es, "yT", [128, 8 * 512], F32)
    yT16 = sb(nc, es, "yT16", [128, 8 * 512], BF16)
    sTt = sb(nc, es, "sTt", [128, 1024], F32)

    for bi in range(n_batch):
        ln_done = {}; copy_done = {}; xnb_tp = {}; stage_dma = {}
        prev_batch_guard = p.milestone("tensor")
        for k in range(32):
            row0 = k * 128
            xbuf, xnb = xt[k % 2], xn16[k % 2]
            lwx = p.dma("sync", xbuf[:], x1_hbm[bi, row0:row0 + 128, :],
                        waits=[ln_done.get(k - 2)])
            p.op("vector", lambda v, xbuf=xbuf: v.bn_stats(
                out=bn6[:, 0:6], in_=xbuf[:, 0:512]), waits=[lwx])
            p.op("vector", lambda v, xbuf=xbuf: v.bn_stats(
                out=bn6[:, 6:12], in_=xbuf[:, 512:1024]))
            agg = p.op("vector", lambda v: v.bn_aggr(out=mv[:], in_=bn6[:]))
            p.op("scalar", lambda sc: sc.activation(
                out=rstd[:], in_=mv[:, 1:2], func=AF.Ln, bias=consts.col(1e-5)),
                waits=[agg, consts.ready])
            ex1 = p.op("scalar", lambda sc: sc.activation(
                out=rstd[:], in_=rstd[:], func=AF.Exp, scale=-0.5))
            p.op("vector", lambda v: v.tensor_scalar(
                out=nmr[:], in0=mv[:, 0:1], scalar1=rstd[:], scalar2=-1.0,
                op0=ALU.mult, op1=ALU.mult), waits=[ex1])
            lnw = p.op("vector", lambda v, xbuf=xbuf: v.tensor_scalar(
                out=xn[:], in0=xbuf[:], scalar1=rstd[:], scalar2=nmr[:],
                op0=ALU.mult, op1=ALU.add), waits=[copy_done.get(k - 1)])
            ln_done[k] = lnw
            x0c_done = p.op("vector", lambda v, k=k: v.tensor_copy(
                out=x0cols[:, k:k + 1], in_=xn[:, 0:1]))
            cpw = p.op("scalar", lambda sc, xnb=xnb: sc.copy(
                out=xnb[:], in_=xn[:]), waits=[lnw, xnb_tp.get(k - 2)])
            copy_done[k] = cpw
            for dc in range(8):
                tu.t(xnb[:, dc * 128:(dc + 1) * 128],
                     xn1T_stage[:, dc * 128:(dc + 1) * 128],
                     bf16=True,
                     waits=[cpw, prev_batch_guard, stage_dma.get(k - 1)])
            xnb_tp[k] = p.milestone("tensor")
            std = tu.done()
            for dc in range(8):
                sd = p.dma("sync",
                           xn1T_hbm[bi, dc * 128:(dc + 1) * 128, row0:row0 + 128],
                           xn1T_stage[:, dc * 128:(dc + 1) * 128], waits=[std])
            stage_dma[k] = sd
        xnt_done = stage_dma[31]
        x0ev = tu.t(x0cols[:], x0T[:], bf16=False, waits=[x0c_done], rows=32,
                    cols=128)
        x0w = p.dma("sync", x0_hbm[bi].rearrange("(a b) -> a b", b=128), x0T[:],
                    waits=[x0ev])
        x0r = p.dma("sync", x0rep[:],
                    x0_hbm[bi].unsqueeze(0).broadcast_to([128, 4096]),
                    waits=[x0w])
        bc_done = None
        xsl_last = {0: None, 1: None}
        for lc in range(8):
            last = None
            for dc in range(8):
                i2 = lc * 8 + dc
                xb = xsl[i2 % 2]
                lwbc = p.dma("sync", xb[:],
                             xn1T_hbm[bi, dc * 128:(dc + 1) * 128,
                                      lc * 512:(lc + 1) * 512],
                             waits=[xnt_done, xsl_last[i2 % 2]])
                last = p.op("tensor", lambda t, xb=xb, dc=dc: t.matmul(
                    out=pBC[0:64, :],
                    lhsT=WxpT[:, dc * 64:(dc + 1) * 64],
                    rhs=xb[:],
                    start=(dc == 0), stop=(dc == 7)),
                    waits=([wxp_rdy, bc_done, lwbc] if dc == 0 else [lwbc]))
                xsl_last[i2 % 2] = last
            bc_done = p.op("scalar", lambda sc, lc=lc: sc.activation(
                out=BCT[:, lc * 512:(lc + 1) * 512], in_=pBC[0:64, :],
                func=AF.Identity, bias=bxp_pad[:]),
                waits=[last, bxp_rdy])
        bct_w = p.dma("sync", bct_hbm[bi], BCT[:], waits=[bc_done])
        if dbg is not None:
            p.dma("sync", dbg["bct"][bi], BCT[:], waits=[bc_done])
        if dbg is not None and "x0r" in dbg:
            p.dma("sync", dbg["x0r"][bi], x0rep[:], waits=[x0r])
        cz = p.op("vector", lambda v: v.memset(carry[:], 0.0))
        rep_last = {0: None, 1: None}
        da_last = {0: None, 1: None}
        sp_evac = {}
        for c in range(n_chunk):
            l0 = c * 512
            for dg in range(8):
                p.op("vector", lambda v, dg=dg, l0=l0: v.tensor_scalar(
                    out=zt[:], in0=x0rep[:, l0:l0 + 512],
                    scalar1=wdt_col[:, dg:dg + 1], scalar2=bdt_col[:, dg:dg + 1],
                    op0=ALU.mult, op1=ALU.add),
                    waits=[x0r, wdt_rdy, bdt_rdy] if (dg == 0 and c == 0) else [])
                abw = p.op("scalar", lambda sc: sc.activation(
                    out=azt[:], in_=zt[:], func=AF.Abs),
                    waits=[p.milestone("vector")])
                p.op("scalar", lambda sc: sc.activation(
                    out=azt[:], in_=azt[:], func=AF.Exp, scale=-1.0))
                lnp = p.op("scalar", lambda sc: sc.activation(
                    out=azt[:], in_=azt[:], func=AF.Ln, bias=consts.col(1.0)))
                p.op("vector", lambda v: v.tensor_scalar(
                    out=relz[:], in0=zt[:], scalar1=0.0, scalar2=None,
                    op0=ALU.max))
                dtw = p.op("vector", lambda v, dg=dg: v.tensor_tensor(
                    out=dtT[:, dg * 512:(dg + 1) * 512], in0=relz[:], in1=azt[:],
                    op=ALU.add), waits=[lnp])
                xb = xsl[dg % 2]
                lwu = p.dma("sync", xb[:],
                            xn1T_hbm[bi, dg * 128:(dg + 1) * 128, l0:l0 + 512],
                            waits=[xnt_done, xsl_last[dg % 2]])
                if dbg is not None and "xsl" in dbg:
                    p.dma("sync", dbg["xsl"][bi, dg * 128:(dg + 1) * 128,
                                             l0:l0 + 512], xb[:], waits=[lwu])
                xsl_last[dg % 2] = p.op("vector",
                                   lambda v, dg=dg, xb=xb: v.tensor_tensor(
                    out=uT[:, dg * 512:(dg + 1) * 512],
                    in0=dtT[:, dg * 512:(dg + 1) * 512],
                    in1=xb[:], op=ALU.mult), waits=[lwu])
            dtu_done = p.milestone("vector")
            for n in range(16):
                bb, cb = Brep[n % 2], Crep[n % 2]
                bw = p.dma("sync", bb[:],
                           bct_hbm[bi, n, l0:l0 + 512].unsqueeze(0)
                           .broadcast_to([128, 512]),
                           waits=[bct_w, rep_last[n % 2]])
                cw2 = p.dma("sync", cb[:],
                            bct_hbm[bi, 32 + n, l0:l0 + 512].unsqueeze(0)
                            .broadcast_to([128, 512]),
                            waits=[bct_w, rep_last[n % 2]])
                for dg in range(8):
                    j = n * 8 + dg
                    da = dA_t[j % 2]
                    daw = p.op("scalar", lambda sc, da=da, dg=dg, n=n:
                               sc.activation(
                        out=da[:], in_=dtT[:, dg * 512:(dg + 1) * 512],
                        func=AF.Exp, scale=A_sb[:, dg * 16 + n:dg * 16 + n + 1]),
                        waits=[dtu_done, A_rdy, da_last[j % 2]] + _allm())
                    p.op("vector", lambda v, dg=dg, bb=bb: v.tensor_tensor(
                        out=W_t[:], in0=uT[:, dg * 512:(dg + 1) * 512],
                        in1=bb[:], op=ALU.mult), waits=[bw, daw] + _allm())
                    da_last[j % 2] = p.op("vector", lambda v, da=da, dg=dg, n=n:
                                       v.tensor_tensor_scan(
                        out=state[:], data0=da[:], data1=W_t[:],
                        initial=carry[:, dg * 16 + n:dg * 16 + n + 1],
                        op0=ALU.mult, op1=ALU.add), waits=_allm())
                    p.op("vector", lambda v, dg=dg, n=n: v.tensor_copy(
                        out=carry[:, dg * 16 + n:dg * 16 + n + 1],
                        in_=state[:, 511:512]), waits=_allm())
                    if n == 0:
                        p.op("vector", lambda v, dg=dg, cb=cb: v.tensor_tensor(
                            out=yT[:, dg * 512:(dg + 1) * 512], in0=state[:],
                            in1=cb[:], op=ALU.mult),
                            waits=[cw2, sp_evac.get((c - 1, 3))] + _allm())
                    else:
                        p.op("vector", lambda v, cb=cb: v.tensor_tensor(
                            out=tmpy[:], in0=state[:], in1=cb[:], op=ALU.mult),
                            waits=[cw2] + _allm())
                        p.op("vector", lambda v, dg=dg: v.tensor_tensor(
                            out=yT[:, dg * 512:(dg + 1) * 512],
                            in0=yT[:, dg * 512:(dg + 1) * 512], in1=tmpy[:],
                            op=ALU.add), waits=_allm())
                rep_last[n % 2] = p.milestone("vector")
            if dbg is not None and "yt" in dbg:
                ydw = p.milestone("vector")
                for dg in range(8):
                    p.dma("sync", dbg["yt"][bi, dg * 128:(dg + 1) * 128, l0:l0 + 512],
                          yT[:, dg * 512:(dg + 1) * 512], waits=[ydw])
                    p.dma("sync", dbg["dtt"][bi, dg * 128:(dg + 1) * 128, l0:l0 + 512],
                          uT[:, dg * 512:(dg + 1) * 512], waits=[ydw])
            ycast = None
            mmguard = p.milestone("tensor")
            for dg in range(8):
                ycast = p.op("vector", lambda v, dg=dg: v.tensor_copy(
                    out=yT16[:, dg * 512:(dg + 1) * 512],
                    in_=yT[:, dg * 512:(dg + 1) * 512]),
                    waits=[mmguard] if dg == 0 else [])
            for ts4 in range(4):
                pm = pSP[ts4 % 2]
                last = None
                for dg in range(8):
                    for oh in range(2):
                        last = p.op("tensor", lambda t, dg=dg, oh=oh, ts4=ts4,
                                    pm=pm: t.matmul(
                            out=pm[oh][:],
                            lhsT=yT16[:, dg * 512 + ts4 * 128: dg * 512 + (ts4 + 1) * 128],
                            rhs=WoT[:, dg * 1024 + oh * 512: dg * 1024 + (oh + 1) * 512],
                            start=(dg == 0), stop=(dg == 7)),
                            waits=([ycast, wo_rdy, sp_evac.get((c, ts4 - 2)),
                                    sp_evac.get((c - 1, ts4 + 2))]
                                   if (dg == 0 and oh == 0) else []))
                ev = None
                for oh in range(2):
                    ev = p.op("vector", lambda v, oh=oh, pm=pm: v.tensor_tensor(
                        out=sTt[:, oh * 512:(oh + 1) * 512], in0=pm[oh][:],
                        in1=bout_rep[:, oh * 512:(oh + 1) * 512], op=ALU.add),
                        waits=([last, bout_rep_rdy,
                                sp_evac.get((c, ts4 - 1)) or sp_evac.get((c - 1, 3))]
                               if oh == 0 else []))
                sp_evac[(c, ts4)] = p.dma(
                    "sync", s_hbm[bi, l0 + ts4 * 128: l0 + (ts4 + 1) * 128, :],
                    sTt[:], waits=[ev])
    return



def emit_L1B(p, nc, es, x1_hbm, xn1T_hbm, s_hbm, rm_hbm, rup_hbm, wc_hbm,
             out_hbm, wc16_hbm, stats_hbm, gpart_hbm, ident, tu, n_batch=2,
             n_tc=16):
    pG = [psbank(nc, es, "l1b_g0"), psbank(nc, es, "l1b_g1")]
    pC = [psbank(nc, es, "l1b_c0"), psbank(nc, es, "l1b_c1"),
          psbank(nc, es, "l1b_c2"), psbank(nc, es, "l1b_c3")]

    def dma_t(out, in_, waits=()):
        s = p.esem["sync"]
        wl = [(w[0], w[1]) for w in waits if w is not None]
        wl.append((p.junk, p.junk.n))
        s.n += 16
        p.ops["sync"].append(
            (lambda e, o=out, i=in_: e.dma_start(out=o, in_=i, transpose=True),
             wl, s, 16))
        p.ops["sync"].append((lambda e: e.nop(), [], p.junk, 1))
        return (s, s.n)

    stats_sb = sb(nc, es, "stats_sb", [128, 128 * 128], BF16)
    es2 = ExitStack()
    wcl = [sb(nc, es2, f"wcl{i}", [128, 4096], F32) for i in range(2)]
    wcl16 = [sb(nc, es2, f"wcl16{i}", [128, 4096], BF16) for i in range(2)]
    cast_prev = [None, None]
    st_prev = [None, None]
    for oc in range(8):
        for kc in range(4):
            i2 = oc * 4 + kc
            wb, wb16 = wcl[i2 % 2], wcl16[i2 % 2]
            lw = p.dma("sync", wb[:],
                       wc_hbm[oc * 128:(oc + 1) * 128, kc * 4096:(kc + 1) * 4096],
                       waits=[cast_prev[i2 % 2]])
            cw = p.op("vector", lambda v, wb=wb, wb16=wb16: v.tensor_copy(
                out=wb16[:], in_=wb[:]), waits=[lw, st_prev[i2 % 2]])
            cast_prev[i2 % 2] = cw
            st_prev[i2 % 2] = p.dma(
                "sync", wc16_hbm[oc * 128:(oc + 1) * 128, kc * 4096:(kc + 1) * 4096],
                wb16[:], waits=[cw])
    wc16_done = p.milestone("sync")
    zt16 = sb(nc, es2, "zt16", [128, 2048], BF16)
    zw = p.op("vector", lambda v: v.memset(zt16[:], 0))
    stz = None
    for i in range(8):
        stz = p.dma("sync", stats_hbm[:, i * 2048:(i + 1) * 2048], zt16[:],
                    waits=[zw])
    RMT = sb(nc, es2, "RMT", [16, 8 * 128], BF16)
    RUPT = sb(nc, es2, "RUPT", [16, 32 * 128], BF16)
    rmld = sb(nc, es2, "rmld", [128, 16], F32)
    rmld16 = sb(nc, es2, "rmld16", [128, 16], BF16)
    gtmp = sb(nc, es2, "gtmp", [16, 64], BF16)
    g_st = {}
    for h in range(4):
        rmt_done = None
        for g8 in range(8):
            lw = p.dma("sync", rmld[:],
                       rm_hbm[h, g8 * 8:(g8 + 1) * 8].rearrange("r i j -> (r i) j"),
                       waits=[rmt_done])
            cw = p.op("vector", lambda v: v.tensor_copy(out=rmld16[:], in_=rmld[:]),
                      waits=[lw])
            rmt_done = tu.t(rmld16[:], RMT[:, g8 * 128:(g8 + 1) * 128], bf16=True,
                            waits=[cw], rows=16, cols=128)
        rupt_done = rmt_done
        for g2 in range(32):
            lw = p.dma("sync", rmld[:],
                       rup_hbm[h, g2 * 2:(g2 + 1) * 2].rearrange("r k j -> (r k) j"),
                       waits=[rupt_done])
            cw = p.op("vector", lambda v: v.tensor_copy(out=rmld16[:], in_=rmld[:]),
                      waits=[lw])
            rupt_done = tu.t(rmld16[:], RUPT[:, g2 * 128:(g2 + 1) * 128], bf16=True,
                             waits=[cw], rows=16, cols=128)
        for r in range(64):
            pgb = pG[r % 2]
            mm = p.op("tensor", lambda t, r=r, pgb=pgb: t.matmul(
                out=pgb[0:16, 0:64],
                lhsT=RMT[:, (r // 8) * 128 + (r % 8) * 16:(r // 8) * 128 + (r % 8) * 16 + 16],
                rhs=RUPT[:, (r // 2) * 128 + (r % 2) * 64:(r // 2) * 128 + (r % 2) * 64 + 64],
                start=True, stop=True),
                waits=[rupt_done, g_st.get((h, r - 2))])
            ev = p.op("scalar", lambda sc, pgb=pgb: sc.copy(out=gtmp[:],
                                                            in_=pgb[0:16, 0:64]),
                      waits=[mm, g_st.get((h, r - 1))])
            pi = h * 32 + r // 2
            lr = 2 * ((r // 2) % 4) + (r % 2)
            col0 = pi * 128 + (r % 2) * 64
            g_st[(h, r)] = p.dma(
                "sync", stats_hbm[lr * 16:(lr + 1) * 16, col0:col0 + 64],
                gtmp[:], waits=[ev, stz])
    gst_done = g_st[(3, 63)]

    stats_rdy = None
    for i in range(8):
        stats_rdy = p.dma("sync", stats_sb[:, i * 2048:(i + 1) * 2048],
                          stats_hbm[:, i * 2048:(i + 1) * 2048],
                          waits=[gst_done])

    es2.close()
    wcA = sb(nc, es, "wcA", [128, 64 * 1024], BF16)
    xsl2 = [sb(nc, es, f"xsl2_{i}", [128, 8 * 256], BF16) for i in range(2)]
    hT = [sb(nc, es, f"hT{i}", [128, 256], BF16) for i in range(2)]
    gst2 = [sb(nc, es, f"gst2_{i}", [128, 1024], F32) for i in range(2)]
    gab = [sb(nc, es, f"gab{i}", [128, 1024], F32) for i in range(2)]
    ssl = [sb(nc, es, f"ssl{i}", [128, 1024], F32) for i in range(2)]
    x1sl = [sb(nc, es, f"x1sl{i}", [128, 1024], F32) for i in range(2)]

    xsl2_last = {0: None, 1: None}
    gab_last = {0: None, 1: None}
    ssl_last = {0: None, 1: None}
    x1_last = {0: None, 1: None}
    gst_last = {0: None, 1: None}
    zps_last = {0: None, 1: None}
    ht_last = {0: None, 1: None}
    ot_st = {}
    gstage_st = None
    wca_last = None
    gT_free = None
    for ph in range(2):
        wca_done = None
        for pl in range(64):
            pg = ph * 64 + pl
            wca_done = dma_t(wcA[:, pl * 1024:(pl + 1) * 1024],
                             wc16_hbm[0:1024, pg * 128:(pg + 1) * 128],
                             waits=([wc16_done, gst_done, stats_rdy, wca_last]
                                    if pl == 0 else []))
        gpw = p.milestone("sync") if ph == 1 else None
        for bi in range(n_batch):
            for tc in range(n_tc):
                t0 = tc * 256
                xb = xsl2[tc % 2]
                xw = None
                for dg in range(8):
                    xw = p.dma("sync", xb[:, dg * 256:(dg + 1) * 256],
                               xn1T_hbm[bi, dg * 128:(dg + 1) * 128, t0:t0 + 256],
                               waits=[xsl2_last[tc % 2]] if dg == 0 else [])
                mm_last = None
                for pl in range(64):
                    pg = ph * 64 + pl
                    dgb = (pg % 32) // 4
                    zb = pG[pl % 2]
                    mm = p.op("tensor", lambda t, pg=pg, dgb=dgb, zb=zb,
                              xb=xb: t.matmul(
                        out=zb[:, 0:256],
                        lhsT=stats_sb[:, pg * 128:(pg + 1) * 128],
                        rhs=xb[:, dgb * 256:(dgb + 1) * 256],
                        start=True, stop=True),
                        waits=([stats_rdy, wca_done, xw, gT_free,
                                zps_last[pl % 2]] if pl < 2
                               else [zps_last[pl % 2]]))
                    hb = hT[pl % 2]
                    gw = p.op("scalar", lambda sc, zb=zb, hb=hb: sc.activation(
                        out=hb[:], in_=zb[:, 0:256], func=AF.Gelu),
                        waits=[mm, ht_last[pl % 2]])
                    zps_last[pl % 2] = gw
                    last = None
                    for ts2 in range(2):
                        for oh in range(2):
                            last = p.op("tensor", lambda t, pl=pl, ts2=ts2,
                                        oh=oh, hb=hb: t.matmul(
                                out=pC[ts2 * 2 + oh][:],
                                lhsT=hb[:, ts2 * 128:(ts2 + 1) * 128],
                                rhs=wcA[:, pl * 1024 + oh * 512:
                                        pl * 1024 + (oh + 1) * 512],
                                start=(pl == 0), stop=(pl == 63)),
                                waits=([gw] if (ts2 == 0 and oh == 0) else []))
                    ht_last[pl % 2] = last
                    mm_last = last
                allmm = mm_last
                xsl2_last[tc % 2] = allmm
                if ph == 0:
                    for ts2 in range(2):
                        gs = gst2[ts2]
                        ev = None
                        for oh in range(2):
                            ev = p.op("scalar", lambda sc, ts2=ts2, oh=oh,
                                      gs=gs: sc.copy(
                                out=gs[:, oh * 512:(oh + 1) * 512],
                                in_=pC[ts2 * 2 + oh][:]),
                                waits=([allmm, gst_last[ts2]] if oh == 0 else []))
                        gst_last[ts2] = p.dma(
                            "sync",
                            gpart_hbm[bi, t0 + ts2 * 128:t0 + (ts2 + 1) * 128, :],
                            gs[:], waits=[ev])
                    gT_free = ev
                else:
                    silw = None
                    for ts2 in range(2):
                        gb = gab[ts2]
                        gld = p.dma("sync", gb[:],
                                    gpart_hbm[bi, t0 + ts2 * 128:t0 + (ts2 + 1) * 128, :],
                                    waits=[gab_last[ts2], gpw])
                        addw = None
                        for oh in range(2):
                            addw = p.op("vector", lambda v, ts2=ts2, oh=oh,
                                        gb=gb: v.tensor_tensor(
                                out=gb[:, oh * 512:(oh + 1) * 512],
                                in0=pC[ts2 * 2 + oh][:],
                                in1=gb[:, oh * 512:(oh + 1) * 512], op=ALU.add),
                                waits=([allmm, gld] if oh == 0 else []))
                        silw = p.op("scalar", lambda sc, gb=gb: sc.activation(
                            out=gb[:], in_=gb[:], func=AF.Silu), waits=[addw])
                        gab_last[ts2] = silw
                    gT_free = silw
                    for ts2 in range(2):
                        gb = gab[ts2]
                        sl = ssl[ts2]
                        x1b = x1sl[ts2]
                        sld = p.dma("sync", sl[:],
                                    s_hbm[bi, t0 + ts2 * 128:t0 + (ts2 + 1) * 128, :],
                                    waits=[ssl_last[ts2]])
                        x1w = p.dma("sync", x1b[:],
                                    x1_hbm[bi, t0 + ts2 * 128:t0 + (ts2 + 1) * 128, :],
                                    waits=[x1_last[ts2]])
                        sgw = p.op("vector", lambda v, gb=gb, sl=sl: v.tensor_tensor(
                            out=sl[:], in0=sl[:], in1=gb[:], op=ALU.mult),
                            waits=[gab_last[ts2], sld])
                        gab_last[ts2] = sgw
                        ow = p.op("vector", lambda v, sl=sl, x1b=x1b: v.tensor_tensor(
                            out=x1b[:], in0=x1b[:], in1=sl[:], op=ALU.add),
                            waits=[sgw, x1w])
                        ssl_last[ts2] = ow
                        x1_last[ts2] = p.dma(
                            "sync", out_hbm[bi, t0 + ts2 * 128:t0 + (ts2 + 1) * 128, :],
                            x1b[:], waits=[ow])
        wca_last = p.milestone("tensor")
    return
import ml_dtypes
from concourse.bass_utils import run_bass_kernel_spmd

NB = 2


def _build_fused():
    nc = bass.Bass("TRN2", target_bir_lowering=False, debug=False)
    ap = lambda name, shape, dt, kind: nc.dram_tensor(name, shape, dt, kind=kind).ap()
    x = ap("x", [NB, 4096, 1024], F32, "ExternalInput")
    W = ap("W", [1024, 1024], F32, "ExternalInput")
    b = ap("b", [1024], F32, "ExternalInput")
    fw = ap("fw", [1024, 1024], F32, "ExternalInput")
    fb = ap("fb", [1024], F32, "ExternalInput")
    al = ap("Alog", [1024, 16], F32, "ExternalInput")
    wdt = ap("Wdt", [1024, 1], F32, "ExternalInput")
    bdt = ap("bdt", [1024], F32, "ExternalInput")
    wxp = ap("Wxp", [32, 1024], F32, "ExternalInput")
    bxp = ap("bxp", [32], F32, "ExternalInput")
    wo = ap("Wout", [1024, 1024], F32, "ExternalInput")
    bo = ap("bout", [1024], F32, "ExternalInput")
    rm = ap("rm", [4, 64, 16, 16], F32, "ExternalInput")
    rup = ap("rup", [4, 64, 64, 16], F32, "ExternalInput")
    wc = ap("wc", [1024, 16384], F32, "ExternalInput")
    out = ap("out", [NB, 4096, 1024], F32, "ExternalOutput")
    fT = ap("fT", [NB, 1024, 4096], BF16, "Internal")
    f = ap("f", [NB, 4096, 1024], BF16, "Internal")
    xd = ap("xd", [NB, 4096, 1024], BF16, "Internal")
    xs = ap("xs", [NB, 4096, 32], F32, "Internal")
    x1 = ap("x1", [NB, 4096, 1024], F32, "Internal")
    xn1T = ap("xn1T", [NB, 1024, 4096], BF16, "Internal")
    sT = ap("sT", [NB, 4096, 1024], F32, "Internal")
    x0s = ap("x0s", [NB, 4096], F32, "Internal")
    bcts = ap("bcts", [NB, 64, 4096], F32, "Internal")
    wc16 = ap("wc16s", [1024, 16384], BF16, "Internal")
    stats = ap("statss", [128, 16384], BF16, "Internal")
    gp = ap("gparts", [NB, 4096, 1024], F32, "Internal")
    es = ExitStack()
    with es, nc.Block() as block:
        p = Prog(nc, es)
        ident = Ident(p, nc, es)
        tu = TUnit(p, nc, es, ident)
        consts = Consts(p, nc, es, [1e-5, 1.0])
        with ExitStack() as es1:
            emit_L0A(p, nc, es1, x, W, b, fT, f, xd, xs, ident, tu, consts,
                     n_batch=NB)
            p.fence()
        with ExitStack() as es2:
            emit_L0B(p, nc, es2, x, fT, f, xd, fw, fb, x1, ident, tu,
                     n_batch=NB)
            p.fence()
        with ExitStack() as es3:
            emit_L1A(p, nc, es3, x1, al, wdt, bdt, wxp, bxp, wo, bo, xn1T, sT,
                     x0s, bcts, ident, tu, consts, n_batch=NB)
            p.fence()
        with ExitStack() as es4:
            emit_L1B(p, nc, es4, x1, xn1T, sT, rm, rup, wc, out, wc16, stats,
                     gp, ident, tu, n_batch=NB)
            p.fence()
        p.emit(block)
    return nc


def kernel(L0_attn_x, L0_attn_W, L0_attn_b, L0_ffn_weight, L0_ffn_bias,
           L1_attn_A_log, L1_attn_W_dt, L1_attn_b_dt, L1_attn_W_xp,
           L1_attn_b_xp, L1_attn_W_out, L1_attn_b_out, L1_ffn_rail_maker,
           L1_ffn_rail_up_proj, L1_ffn_w_cproj):
    cores = list(range(8))
    nc = _build_fused()
    maps = [{"x": np.ascontiguousarray(L0_attn_x[2 * c:2 * c + 2]),
             "W": L0_attn_W, "b": L0_attn_b, "fw": L0_ffn_weight,
             "fb": L0_ffn_bias, "Alog": L1_attn_A_log, "Wdt": L1_attn_W_dt,
             "bdt": L1_attn_b_dt, "Wxp": L1_attn_W_xp, "bxp": L1_attn_b_xp,
             "Wout": L1_attn_W_out, "bout": L1_attn_b_out,
             "rm": L1_ffn_rail_maker, "rup": L1_ffn_rail_up_proj,
             "wc": L1_ffn_w_cproj} for c in cores]
    res = run_bass_kernel_spmd(nc, maps, core_ids=cores).results
    return np.concatenate([np.asarray(res[c]["out"], dtype=np.float32)
                           for c in cores], axis=0)


# version 11 (final)
# speedup vs baseline: 1.3913x; 1.3913x over previous
import numpy as np
from contextlib import ExitStack
import concourse.bass as bass
import concourse.mybir as mybir

F32 = mybir.dt.float32
BF16 = mybir.dt.bfloat16
I32 = mybir.dt.int32
AF = mybir.ActivationFunctionType
ALU = mybir.AluOpType
AX = mybir.AxisListType

MAGIC = 12582912.0


class Sem:
    __slots__ = ("h", "n")
    def __init__(self, nc, es, name):
        self.h = es.enter_context(nc.semaphore(name))
        self.n = 0


class Prog:
    ENGINES = ("sync", "vector", "scalar", "tensor", "gpsimd")
    CHAINED = ("vector", "scalar", "gpsimd")

    def __init__(self, nc, es):
        self.nc, self.es = nc, es
        self.ops = {e: [] for e in self.ENGINES}
        self._epoch = 0
        self.esem = {e: Sem(nc, es, f"e_{e}_0") for e in self.ENGINES}
        self.junk = Sem(nc, es, "junk")
        self.dmasems = {}

    @staticmethod
    def _flat(waits):
        out = []
        for w in waits:
            if w is None:
                continue
            if isinstance(w, list):
                out.extend(x for x in w if x is not None)
            else:
                out.append(w)
        return out

    def op(self, eng, fn, waits=()):
        s = self.esem[eng]
        wl = [(w[0], w[1]) for w in self._flat(waits)]
        if eng in self.CHAINED and s.n > 0:
            wl.append((s, s.n))
        s.n += 1
        self.ops[eng].append((fn, wl, s, 1))
        return (s, s.n)

    def _dmasem(self, out):
        name = out.tensor.name
        s = self.dmasems.get(name)
        if s is None:
            s = Sem(self.nc, self.es, f"d_{name}"[:31])
            self.dmasems[name] = s
        return s

    def dma(self, eng, out, in_, waits=()):
        s = self._dmasem(out)
        wl = [(w[0], w[1]) for w in self._flat(waits)]
        wl.append((self.junk, self.junk.n))
        s.n += 16
        self.ops[eng].append(
            (lambda e, o=out, i=in_: e.dma_start(out=o, in_=i), wl, s, 16))
        self.ops[eng].append((lambda e: e.nop(), [], self.junk, 1))
        return (s, s.n)

    def dma_barrier(self):
        return [(s, s.n) for s in self.dmasems.values() if s.n > 0]

    def milestone(self, eng):
        s = self.esem[eng]
        return (s, s.n)

    def fence(self):
        ms = ([(self.esem[e], self.esem[e].n) for e in self.ENGINES
               if self.esem[e].n > 0] + self.dma_barrier())
        for e in self.ENGINES:
            self.op(e, lambda eng: eng.nop(), waits=ms)
        self._epoch += 1
        self.esem = {e: Sem(self.nc, self.es, f"e_{e}_{self._epoch}")
                     for e in self.ENGINES}

    def emit(self, block):
        for e in self.ENGINES:
            ops = self.ops[e]
            if not ops:
                continue
            def body(eng, ops=ops):
                for fn, wl, inc, inck in ops:
                    seen = {}
                    for (s, v) in wl:
                        if v > 0 and seen.get(id(s), -1) < v:
                            seen[id(s)] = v
                            eng.wait_ge(s.h, v)
                    instr = fn(eng)
                    instr.then_inc(inc.h, inck)
            getattr(block, e)(body)


def sb(nc, es, name, shape, dtype):
    return es.enter_context(nc.sbuf_tensor(name, shape, dtype))


def psbank(nc, es, name):
    return es.enter_context(nc.psum_tensor(name, [128, 512], F32))


class Ident:
    def __init__(self, p, nc, es):
        iot = sb(nc, es, "idn_iot", [128, 128], I32)
        self.b16 = sb(nc, es, "ident_bf", [128, 128], BF16)
        self.f32 = sb(nc, es, "ident_f32", [128, 128], F32)
        w = p.op("gpsimd", lambda g: g.iota(out=iot[:], pattern=[[-1, 128]],
                                            base=0, channel_multiplier=1))
        p.op("vector", lambda v: v.tensor_scalar(out=self.b16[:], in0=iot[:],
                                                 scalar1=0, scalar2=None,
                                                 op0=ALU.is_equal), waits=[w])
        self.ones1 = sb(nc, es, "ones1", [1, 128], F32)
        p.op("vector", lambda v: v.memset(self.ones1[:], 1.0))
        self.ready = p.op("vector", lambda v: v.tensor_scalar(
            out=self.f32[:], in0=iot[:], scalar1=0, scalar2=None,
            op0=ALU.is_equal), waits=[w])


class TUnit:
    def __init__(self, p, nc, es, ident, name="tu"):
        self.p, self.ident = p, ident
        self.banks = [psbank(nc, es, f"{name}_a"), psbank(nc, es, f"{name}_b")]
        self.evs = [None, None]
        self._k = 0

    def t(self, src_ap, dst_ap, bf16=True, waits=(), rows=None, cols=None):
        p = self.p
        if rows is None:
            rows = 128
        if cols is None:
            cols = 128
        bank_i = self._k % 2
        self._k += 1
        bank = self.banks[bank_i]
        view = (bank[0:rows, 0:(cols + 1) // 2].bitcast(BF16)[:, 0:cols]
                if bf16 else bank[0:rows, 0:cols])
        idn = self.ident.b16 if bf16 else self.ident.f32
        w = [self.ident.ready] + list(waits)
        if self.evs[bank_i] is not None:
            w.append(self.evs[bank_i])
        tw = p.op("tensor", lambda t, s=src_ap, vv=view, idn=idn, n=cols:
                  t.transpose(out=vv, in_=s, identity=idn[0:n, 0:n]), waits=w)
        ev = p.op("scalar", lambda sc, d=dst_ap, vv=view: sc.copy(out=d, in_=vv),
                  waits=[tw])
        self.evs[bank_i] = ev
        return ev

    def done(self):
        return self.p.milestone("scalar")


class Consts:
    def __init__(self, p, nc, es, values):
        self.tile = sb(nc, es, "constcols", [128, len(values)], F32)
        self.idx = {}
        w = None
        for i, v in enumerate(values):
            self.idx[v] = i
            w = p.op("vector", lambda ve, i=i, v=v: ve.memset(
                self.tile[:, i:i + 1], v))
        self.ready = w

    def col(self, v):
        return self.tile[:, self.idx[v]:self.idx[v] + 1]


def bias_cols(p, nc, es, tu, name, hbm_vec, n):
    rowt = sb(nc, es, name + "_row", [n, 128], F32)
    colt = sb(nc, es, name + "_col", [128, n], F32)
    lw = p.dma("sync", rowt[:], hbm_vec.rearrange("(a b) -> a b", b=128))
    return colt, tu.t(rowt[:], colt[:], bf16=False, waits=[lw], cols=n)


def quant_tile(p, bufs, src_ap, out16_ap, scs_ap, waits=()):
    am, rs, vq, mg, qq = bufs["am"], bufs["rs"], bufs["vq"], bufs["mg"], bufs["qq"]
    x3 = src_ap.rearrange("p (b i) -> p b i", i=32)
    p.op("vector", lambda v, x3=x3: v.tensor_reduce(
        out=am[:], in_=x3, axis=AX.X, op=ALU.max, apply_absolute_value=True),
        waits=waits)
    p.op("vector", lambda v, scs_ap=scs_ap: v.tensor_scalar(
        out=scs_ap, in0=am[:], scalar1=1.0 / 7.5, scalar2=1e-30,
        op0=ALU.mult, op1=ALU.max))
    p.op("vector", lambda v, scs_ap=scs_ap: v.reciprocal(out=rs[:], in_=scs_ap))
    rsb = rs[:].unsqueeze(2).broadcast_to([128, 32, 32])
    p.op("vector", lambda v, x3=x3, rsb=rsb: v.tensor_tensor(
        out=vq[:].rearrange("p (b i) -> p b i", i=32), in0=x3, in1=rsb,
        op=ALU.mult))
    p.op("vector", lambda v: v.tensor_scalar(
        out=mg[:].bitcast(I32), in0=vq[:].bitcast(I32), scalar1=0x7f800000,
        scalar2=None, op0=ALU.bitwise_and))
    p.op("vector", lambda v: v.tensor_scalar(
        out=mg[:], in0=mg[:], scalar1=MAGIC * 0.125, scalar2=MAGIC * 0.125,
        op0=ALU.mult, op1=ALU.max))
    p.op("vector", lambda v: v.tensor_scalar(
        out=mg[:], in0=mg[:], scalar1=MAGIC * 0.5, scalar2=None, op0=ALU.min))
    p.op("vector", lambda v: v.tensor_tensor(out=qq[:], in0=vq[:], in1=mg[:],
                                             op=ALU.add))
    p.op("vector", lambda v: v.tensor_tensor(out=qq[:], in0=qq[:], in1=mg[:],
                                             op=ALU.subtract))
    scb = scs_ap.unsqueeze(2).broadcast_to([128, 32, 32])
    return p.op("vector", lambda v, out16_ap=out16_ap, scb=scb: v.tensor_tensor(
        out=out16_ap.rearrange("p (b i) -> p b i", i=32),
        in0=qq[:].rearrange("p (b i) -> p b i", i=32), in1=scb, op=ALU.mult))


def mk_quant_bufs(nc, es, pfx):
    return {
        "am": sb(nc, es, pfx + "_am", [128, 32], F32),
        "rs": sb(nc, es, pfx + "_rs", [128, 32], F32),
        "vq": sb(nc, es, pfx + "_vq", [128, 1024], F32),
        "mg": sb(nc, es, pfx + "_mg", [128, 1024], F32),
        "qq": sb(nc, es, pfx + "_qq", [128, 1024], F32),
    }



def emit_L0A(p, nc, es, x_hbm, W_hbm, b_hbm, fT_hbm, f_hbm, xd_hbm, xs_hbm,
             ident, tu, consts, n_batch=2):
    pMM = [psbank(nc, es, "l0a_mm0"), psbank(nc, es, "l0a_mm1")]

    b_col, b_rdy = bias_cols(p, nc, es, tu, "fb", b_hbm, 8)

    Wrow = [sb(nc, es, f"Wrow{i}", [128, 1024], F32) for i in range(2)]
    Wrow16 = [sb(nc, es, f"Wrow16{i}", [128, 1024], BF16) for i in range(2)]
    WT_sb = sb(nc, es, "WT", [128, 8 * 1024], BF16)
    cast_done = [None, None]
    tp_done = [None, None]
    for hc in range(8):
        wb, wb16 = Wrow[hc % 2], Wrow16[hc % 2]
        lw = p.dma("sync", wb[:], W_hbm[hc * 128:(hc + 1) * 128, :],
                   waits=[cast_done[hc % 2]])
        cw = p.op("vector", lambda v, wb=wb, wb16=wb16: v.tensor_copy(
            out=wb16[:], in_=wb[:]), waits=[lw, tp_done[hc % 2]])
        cast_done[hc % 2] = cw
        for dc in range(8):
            tu.t(wb16[:, dc * 128:(dc + 1) * 128],
                 WT_sb[:, dc * 1024 + hc * 128: dc * 1024 + (hc + 1) * 128],
                 bf16=True, waits=[cw])
        tp_done[hc % 2] = p.milestone("tensor")
    wt_rdy = tu.done()

    xt = [sb(nc, es, f"xt{i}", [128, 1024], F32) for i in range(2)]
    bn6 = sb(nc, es, "bn6", [128, 12], F32)
    mv = sb(nc, es, "mv", [128, 2], F32)
    rstd = sb(nc, es, "rstd", [128, 1], F32)
    nmr = sb(nc, es, "nmr", [128, 1], F32)
    xn = sb(nc, es, "xn", [128, 1024], F32)
    xn16 = [sb(nc, es, f"xn16_{i}", [128, 1024], BF16) for i in range(2)]
    scs = [sb(nc, es, f"scs{i}", [128, 32], F32) for i in range(2)]
    xd16 = [sb(nc, es, f"xd16_{i}", [128, 1024], BF16) for i in range(2)]
    qb = mk_quant_bufs(nc, es, "l0a")
    xnT = sb(nc, es, "xnT", [128, 8 * 512], BF16)
    fTs = sb(nc, es, "fTs", [128, 8 * 512], BF16)
    fnt = sb(nc, es, "fnt", [128, 1024], BF16)

    ln_done = {}; copy_done = {}; xnb_tp = {}; xd_dma = {}
    evac_done = {}; ft_dma = {}; chunk_mm = {}; f_dma = {}

    for bi in range(n_batch):
        for sc_i in range(8):
            prev_key = (bi, sc_i - 1) if sc_i else (bi - 1, 7)
            for ti in range(4):
                k = bi * 32 + sc_i * 4 + ti
                row0 = (sc_i * 4 + ti) * 128
                xbuf, xdb, xnb, scb_t = xt[k % 2], xd16[k % 2], xn16[k % 2], scs[k % 2]
                lw = p.dma("sync", xbuf[:], x_hbm[bi, row0:row0 + 128, :],
                           waits=[ln_done.get(k - 2)])
                p.op("vector", lambda v, xbuf=xbuf: v.bn_stats(
                    out=bn6[:, 0:6], in_=xbuf[:, 0:512]), waits=[lw])
                p.op("vector", lambda v, xbuf=xbuf: v.bn_stats(
                    out=bn6[:, 6:12], in_=xbuf[:, 512:1024]))
                agg = p.op("vector", lambda v: v.bn_aggr(out=mv[:], in_=bn6[:]))
                p.op("scalar", lambda sc: sc.activation(
                    out=rstd[:], in_=mv[:, 1:2], func=AF.Ln,
                    bias=consts.col(1e-5)), waits=[agg, consts.ready])
                ex1 = p.op("scalar", lambda sc: sc.activation(
                    out=rstd[:], in_=rstd[:], func=AF.Exp, scale=-0.5))
                p.op("vector", lambda v: v.tensor_scalar(
                    out=nmr[:], in0=mv[:, 0:1], scalar1=rstd[:], scalar2=-1.0,
                    op0=ALU.mult, op1=ALU.mult), waits=[ex1])
                lnw = p.op("vector", lambda v, xbuf=xbuf: v.tensor_scalar(
                    out=xn[:], in0=xbuf[:], scalar1=rstd[:], scalar2=nmr[:],
                    op0=ALU.mult, op1=ALU.add), waits=[copy_done.get(k - 1)])
                ln_done[k] = lnw
                qw = quant_tile(p, qb, xn[:], xdb[:], scb_t[:],
                                waits=[xd_dma.get(k - 2)])
                p.dma("sync", xd_hbm[bi, row0:row0 + 128, :], xdb[:], waits=[qw])
                xd_dma[k] = p.dma("sync", xs_hbm[bi, row0:row0 + 128, :],
                                  scb_t[:], waits=[qw])
                cpw = p.op("scalar", lambda sc, xnb=xnb: sc.copy(
                    out=xnb[:], in_=xn[:]), waits=[lnw, xnb_tp.get(k - 2)])
                copy_done[k] = cpw
                extra = [cpw]
                if ti == 0 and k > 0:
                    extra.append(chunk_mm.get(prev_key))
                for dc in range(8):
                    tu.t(xnb[:, dc * 128:(dc + 1) * 128],
                         xnT[:, dc * 512 + ti * 128: dc * 512 + (ti + 1) * 128],
                         bf16=True, waits=(extra if dc == 0 else [cpw]))
                xnb_tp[k] = p.milestone("tensor")
            tw = tu.done()
            for hc in range(8):
                pm = pMM[hc % 2]
                psum_free = evac_done.get((bi, sc_i, hc - 2))
                if psum_free is None and (bi, sc_i) != (0, 0):
                    psum_free = evac_done.get(prev_key + (6 + hc,))
                last = None
                for dc in range(8):
                    last = p.op("tensor", lambda t, hc=hc, dc=dc, pm=pm: t.matmul(
                        out=pm[:],
                        lhsT=WT_sb[:, dc * 1024 + hc * 128:dc * 1024 + (hc + 1) * 128],
                        rhs=xnT[:, dc * 512:(dc + 1) * 512],
                        start=(dc == 0), stop=(dc == 7)),
                        waits=([tw, wt_rdy, psum_free] if dc == 0 else []))
                ftd = (ft_dma.get(prev_key + (hc,))
                       if (bi, sc_i) != (0, 0) else None)
                ev = p.op("scalar", lambda sc, hc=hc, pm=pm: sc.activation(
                    out=fTs[:, hc * 512:(hc + 1) * 512], in_=pm[:],
                    func=AF.Identity, bias=b_col[:, hc:hc + 1]),
                    waits=[last, b_rdy, ftd])
                evac_done[(bi, sc_i, hc)] = ev
                ft_dma[(bi, sc_i, hc)] = p.dma(
                    "sync", fT_hbm[bi, hc * 128:(hc + 1) * 128,
                                   sc_i * 512:(sc_i + 1) * 512],
                    fTs[:, hc * 512:(hc + 1) * 512], waits=[ev])
            chunk_mm[(bi, sc_i)] = p.milestone("tensor")
            fpw = p.milestone("scalar")
            for q in range(4):
                for hc in range(8):
                    w = [fpw]
                    if q > 0 and hc == 0:
                        w.append(f_dma.get((bi, sc_i, q - 1)))
                    tu.t(fTs[:, hc * 512 + q * 128: hc * 512 + (q + 1) * 128],
                         fnt[:, hc * 128:(hc + 1) * 128], bf16=True, waits=w)
                f_dma[(bi, sc_i, q)] = p.dma(
                    "sync",
                    f_hbm[bi, sc_i * 512 + q * 128: sc_i * 512 + (q + 1) * 128, :],
                    fnt[:], waits=[tu.done()])
    return



def emit_wd_quant(p, nc, es, w_hbm, tu, wdT_sb, qb):
    wt = sb(nc, es, "q_wt", [128, 1024], F32)
    wd16 = sb(nc, es, "q_wd16", [128, 1024], BF16)
    scs = sb(nc, es, "q_scs", [128, 32], F32)
    prev_tp = None
    for oc in range(8):
        lw = p.dma("sync", wt[:], w_hbm[oc * 128:(oc + 1) * 128, :],
                   waits=[prev_tp])
        qw = quant_tile(p, qb, wt[:], wd16[:], scs[:], waits=[lw])
        for dc in range(8):
            tu.t(wd16[:, dc * 128:(dc + 1) * 128],
                 wdT_sb[:, dc * 1024 + oc * 128: dc * 1024 + (oc + 1) * 128],
                 bf16=True, waits=[qw])
        prev_tp = p.milestone("tensor")
    return tu.done()


def emit_band(p, nc, es, band_sb):
    bi32 = sb(nc, es, "band_i32", [128, 3 * 128], I32)
    w = None
    for i, off in enumerate((-128, 0, 128)):
        w = p.op("gpsimd", lambda g, i=i, off=off: g.iota(
            out=bi32[:, i * 128:(i + 1) * 128], pattern=[[1, 128]], base=off,
            channel_multiplier=-1))
    p.op("scalar", lambda sc: sc.activation(out=band_sb[:], in_=bi32[:],
                                            func=AF.Abs), waits=[w])
    p.op("vector", lambda v: v.tensor_scalar(
        out=band_sb[:], in0=band_sb[:], scalar1=10.0, scalar2=None,
        op0=ALU.min), waits=[p.milestone("scalar")])
    return p.op("vector", lambda v: v.tensor_scalar(
        out=band_sb[:], in0=band_sb[:], scalar1=-10.0, scalar2=0.01,
        op0=ALU.add, op1=ALU.mult))


def emit_L0B(p, nc, es, x_hbm, fT_hbm, f_hbm, xd_hbm, fw_hbm, fb_hbm, x1_hbm,
             ident, tu, n_batch=2, dbg=None, n_ib=32):
    pS = [psbank(nc, es, "l0b_s0"), psbank(nc, es, "l0b_s1")]
    pPV = [psbank(nc, es, "l0b_pv0"), psbank(nc, es, "l0b_pv1")]
    pF = [psbank(nc, es, "l0b_f0"), psbank(nc, es, "l0b_f1")]

    qb = mk_quant_bufs(nc, es, "l0b")
    wdT_sb = sb(nc, es, "wdT", [128, 8 * 1024], BF16)
    wd_rdy = emit_wd_quant(p, nc, es, fw_hbm, tu, wdT_sb, qb)
    band_sb = sb(nc, es, "band", [128, 3 * 128], F32)
    band_rdy = emit_band(p, nc, es, band_sb)
    brep = sb(nc, es, "brep", [128, 1024], F32)
    brep_rdy = p.dma("sync", brep[:],
                     fb_hbm.unsqueeze(0).broadcast_to([128, 1024]))

    fTsb = sb(nc, es, "fTsb", [128, 8 * 4096], BF16)
    fsb = sb(nc, es, "fsb", [128, 32 * 1024], BF16)
    P_sb = sb(nc, es, "P_sb", [128, 1024], BF16)
    PT_sb = sb(nc, es, "PT_sb", [128, 8 * 128], BF16)
    a_acc = sb(nc, es, "a_acc", [128, 1024], F32)
    a_sb = sb(nc, es, "a_sb", [128, 1024], F32)
    tm2 = sb(nc, es, "tm2", [128, 2], F32)
    M_run = sb(nc, es, "M_run", [128, 1], F32)
    M_new = sb(nc, es, "M_new", [128, 1], F32)
    negM = sb(nc, es, "negM", [128, 1], F32)
    dM = sb(nc, es, "dM", [128, 1], F32)
    alpha = sb(nc, es, "alpha", [128, 1], F32)
    zp = sb(nc, es, "zp", [128, 2], F32)
    Z_run = sb(nc, es, "Z_run", [128, 1], F32)
    zr = sb(nc, es, "zr", [128, 1], F32)
    xdt = sb(nc, es, "xdt", [128, 1024], BF16)
    xdT = sb(nc, es, "xdT", [128, 8 * 128], BF16)
    g_sb = sb(nc, es, "g_sb", [128, 1024], F32)
    xres = sb(nc, es, "xres", [128, 1024], F32)
    x1t = sb(nc, es, "x1t", [128, 1024], F32)

    for bi in range(n_batch):
        x1_store = p.dma_barrier()
        flw = None
        for dc in range(8):
            flw = p.dma("sync", fTsb[:, dc * 4096:(dc + 1) * 4096],
                        fT_hbm[bi, dc * 128:(dc + 1) * 128, :],
                        waits=[x1_store] if dc == 0 else [])
        for jt in range(32):
            flw = p.dma("sync", fsb[:, jt * 1024:(jt + 1) * 1024],
                        f_hbm[bi, jt * 128:(jt + 1) * 128, :])
        fl_rdy = flw
        prev_a = None
        prev_exp = None
        prev_sil = None
        for ib in range(n_ib):
            i0 = ib * 128
            pv_free = None
            for jsb in range(4):
                for jc in range(2):
                    j0 = jsb * 1024 + jc * 512
                    w0 = [fl_rdy, wd_rdy, band_rdy, brep_rdy, prev_exp]
                    last = None
                    for dc in range(8):
                        last = p.op("tensor", lambda t, dc=dc, jc=jc, j0=j0, i0=i0:
                                    t.matmul(
                            out=pS[jc][:],
                            lhsT=fTsb[:, dc * 4096 + i0: dc * 4096 + i0 + 128],
                            rhs=fTsb[:, dc * 4096 + j0: dc * 4096 + j0 + 512],
                            start=(dc == 0), stop=(dc == 7)),
                            waits=(w0 if dc == 0 else []))
                mmw = p.milestone("tensor")
                for q in range(8):
                    off = jsb * 1024 + q * 128 - i0
                    if off in (-128, 0, 128):
                        bs = (off + 128) // 128
                        jc, qq = divmod(q, 4)
                        p.op("vector", lambda v, jc=jc, qq=qq, bs=bs: v.tensor_tensor(
                            out=pS[jc][:, qq * 128:(qq + 1) * 128],
                            in0=pS[jc][:, qq * 128:(qq + 1) * 128],
                            in1=band_sb[:, bs * 128:(bs + 1) * 128], op=ALU.add),
                            waits=[mmw])
                p.op("vector", lambda v: v.tensor_reduce(
                    out=tm2[:, 0:1], in_=pS[0][:], axis=AX.X, op=ALU.max),
                    waits=[mmw])
                p.op("vector", lambda v: v.tensor_reduce(
                    out=tm2[:, 1:2], in_=pS[1][:], axis=AX.X, op=ALU.max))
                if jsb == 0:
                    p.op("vector", lambda v: v.tensor_scalar(
                        out=M_new[:], in0=tm2[:, 0:1], scalar1=tm2[:, 1:2],
                        scalar2=None, op0=ALU.max))
                else:
                    p.op("vector", lambda v: v.tensor_scalar(
                        out=M_new[:], in0=tm2[:, 0:1], scalar1=tm2[:, 1:2],
                        scalar2=M_run[:], op0=ALU.max, op1=ALU.max))
                    p.op("vector", lambda v: v.tensor_scalar(
                        out=dM[:], in0=M_run[:], scalar1=M_new[:], scalar2=None,
                        op0=ALU.subtract))
                p.op("vector", lambda v: v.tensor_scalar(
                    out=negM[:], in0=M_new[:], scalar1=-1.0, scalar2=None,
                    op0=ALU.mult))
                vew = p.op("vector", lambda v: v.tensor_copy(out=M_run[:],
                                                             in_=M_new[:]))
                if jsb > 0:
                    p.op("scalar", lambda sc: sc.activation(
                        out=alpha[:], in_=dM[:], func=AF.Exp), waits=[vew])
                p.op("scalar", lambda sc: sc.activation(
                    out=P_sb[:, 0:512], in_=pS[0][:], func=AF.Exp, bias=negM[:]),
                    waits=[vew])
                exw = p.op("scalar", lambda sc: sc.activation(
                    out=P_sb[:, 512:1024], in_=pS[1][:], func=AF.Exp,
                    bias=negM[:]))
                prev_exp = exw
                p.op("vector", lambda v: v.tensor_reduce(
                    out=zp[:, 0:1], in_=P_sb[:, 0:512], axis=AX.X, op=ALU.add),
                    waits=[exw])
                zpw = p.op("vector", lambda v: v.tensor_reduce(
                    out=zp[:, 1:2], in_=P_sb[:, 512:1024], axis=AX.X, op=ALU.add))
                if jsb == 0:
                    p.op("vector", lambda v: v.tensor_scalar(
                        out=Z_run[:], in0=zp[:, 0:1], scalar1=zp[:, 1:2],
                        scalar2=None, op0=ALU.add), waits=[zpw])
                else:
                    p.op("vector", lambda v: v.tensor_scalar(
                        out=Z_run[:], in0=Z_run[:], scalar1=alpha[:],
                        scalar2=zp[:, 0:1], op0=ALU.mult, op1=ALU.add),
                        waits=[zpw, p.milestone("scalar")])
                    p.op("vector", lambda v: v.tensor_scalar(
                        out=Z_run[:], in0=Z_run[:], scalar1=zp[:, 1:2],
                        scalar2=None, op0=ALU.add))
                for q in range(8):
                    tu.t(P_sb[:, q * 128:(q + 1) * 128],
                         PT_sb[:, q * 128:(q + 1) * 128], bf16=True, waits=[exw])
                ptw = tu.done()
                last = None
                for jb in range(8):
                    jt = jsb * 8 + jb
                    for oc in range(2):
                        last = p.op("tensor", lambda t, jb=jb, jt=jt, oc=oc:
                                    t.matmul(
                            out=pPV[oc][:],
                            lhsT=PT_sb[:, jb * 128:(jb + 1) * 128],
                            rhs=fsb[:, jt * 1024 + oc * 512:
                                    jt * 1024 + (oc + 1) * 512],
                            start=(jb == 0), stop=(jb == 7)),
                            waits=([ptw, pv_free] if (jb == 0 and oc == 0)
                                   else []))
                pvw = p.milestone("tensor")
                if jsb == 0:
                    p.op("scalar", lambda sc: sc.copy(out=a_acc[:, 0:512],
                                                      in_=pPV[0][:]),
                         waits=[pvw, prev_a])
                    mg2 = p.op("scalar", lambda sc: sc.copy(
                        out=a_acc[:, 512:1024], in_=pPV[1][:]))
                    pv_free = mg2
                else:
                    p.op("vector", lambda v: v.scalar_tensor_tensor(
                        out=a_acc[:, 0:512], in0=a_acc[:, 0:512],
                        scalar=alpha[:], in1=pPV[0][:], op0=ALU.mult,
                        op1=ALU.add), waits=[pvw, p.milestone("scalar")])
                    pv_free = p.op("vector", lambda v: v.scalar_tensor_tensor(
                        out=a_acc[:, 512:1024], in0=a_acc[:, 512:1024],
                        scalar=alpha[:], in1=pPV[1][:], op0=ALU.mult,
                        op1=ALU.add))
            p.op("vector", lambda v: v.reciprocal(out=zr[:], in_=Z_run[:]),
                 waits=[p.milestone("scalar")])
            aw = p.op("vector", lambda v: v.tensor_scalar(
                out=a_sb[:], in0=a_acc[:], scalar1=zr[:], scalar2=None,
                op0=ALU.mult))
            prev_a = aw
            if dbg is not None:
                p.dma("sync", dbg["a"][bi, i0:i0 + 128, :], a_sb[:], waits=[aw])
                p.dma("sync", dbg["z"][bi, i0:i0 + 128, :], Z_run[:], waits=[aw])
                p.dma("sync", dbg["m"][bi, i0:i0 + 128, :], M_run[:], waits=[aw])
            xw = p.dma("sync", xdt[:], xd_hbm[bi, i0:i0 + 128, :],
                       waits=[tu.done()])
            for dc in range(8):
                tu.t(xdt[:, dc * 128:(dc + 1) * 128],
                     xdT[:, dc * 128:(dc + 1) * 128], bf16=True, waits=[xw])
            xtw = tu.done()
            last = None
            for oc in range(2):
                for dc in range(8):
                    last = p.op("tensor", lambda t, oc=oc, dc=dc: t.matmul(
                        out=pF[oc][:],
                        lhsT=xdT[:, dc * 128:(dc + 1) * 128],
                        rhs=wdT_sb[:, dc * 1024 + oc * 512:
                                   dc * 1024 + (oc + 1) * 512],
                        start=(dc == 0), stop=(dc == 7)),
                        waits=([xtw, prev_sil] if (oc == 0 and dc == 0) else []))
            fmw = p.milestone("tensor")
            p.op("vector", lambda v: v.tensor_tensor(
                out=g_sb[:, 0:512], in0=pF[0][:], in1=brep[:, 0:512],
                op=ALU.add), waits=[fmw])
            gw = p.op("vector", lambda v: v.tensor_tensor(
                out=g_sb[:, 512:1024], in0=pF[1][:], in1=brep[:, 512:1024],
                op=ALU.add))
            sil = p.op("scalar", lambda sc: sc.activation(
                out=g_sb[:], in_=g_sb[:], func=AF.Silu), waits=[gw])
            prev_sil = sil
            rw = p.dma("sync", xres[:], x_hbm[bi, i0:i0 + 128, :])
            p.op("vector", lambda v: v.tensor_tensor(
                out=x1t[:], in0=a_sb[:], in1=g_sb[:], op=ALU.mult),
                waits=[sil, rw, aw])
            xw1 = p.op("vector", lambda v: v.tensor_tensor(
                out=x1t[:], in0=x1t[:], in1=xres[:], op=ALU.add))
            p.dma("sync", x1_hbm[bi, i0:i0 + 128, :], x1t[:], waits=[xw1])
    return



def emit_L1A(p, nc, es, x1_hbm, Alog_hbm, Wdt_hbm, bdt_hbm, Wxp_hbm, bxp_hbm,
             Wout_hbm, bout_hbm, xn1T_hbm, s_hbm, x0_hbm, bct_hbm,
             ident, tu, consts, n_batch=2, n_chunk=8, dbg=None):
    import os
    _ser = os.environ.get("SERIALIZE_SCAN", "0") == "1"
    def _allm():
        return [p.milestone(e) for e in ("sync", "vector", "scalar", "tensor")] if _ser else []
    """x0_hbm [nb, 4096] f32 Internal; bct_hbm [nb, 64, 4096] f32 Internal.
    Outputs: xn1T_hbm [nb, 1024, 4096] bf16, sT_hbm [nb, 1024, 4096] f32."""
    pBC = psbank(nc, es, "l1a_bc")
    pSP = [[psbank(nc, es, "l1a_sp0"), psbank(nc, es, "l1a_sp1")],
           [psbank(nc, es, "l1a_sp2"), psbank(nc, es, "l1a_sp3")]]

    A_sb = sb(nc, es, "A_sb", [128, 8 * 16], F32)
    aw = None
    for dg in range(8):
        aw = p.dma("sync", A_sb[:, dg * 16:(dg + 1) * 16],
                   Alog_hbm[dg * 128:(dg + 1) * 128, :])
    p.op("scalar", lambda sc: sc.activation(out=A_sb[:], in_=A_sb[:],
                                            func=AF.Exp), waits=[aw])
    A_rdy = p.op("vector", lambda v: v.tensor_scalar(
        out=A_sb[:], in0=A_sb[:], scalar1=-1.0, scalar2=None, op0=ALU.mult),
        waits=[p.milestone("scalar")])
    wdt_col, wdt_rdy = bias_cols(p, nc, es, tu, "wdt", Wdt_hbm.flatten(), 8)
    bdt_col, bdt_rdy = bias_cols(p, nc, es, tu, "bdt", bdt_hbm, 8)
    bout_row = sb(nc, es, "bout_row", [1, 1024], F32)
    bout_rep = sb(nc, es, "bout_rep", [128, 1024], F32)
    borw = p.dma("sync", bout_row[:], bout_hbm.unsqueeze(0))
    bout_rep_rdy = None
    for oh in range(2):
        bmm = p.op("tensor", lambda t, oh=oh: t.matmul(
            out=tu.banks[0][:], lhsT=ident.ones1[:],
            rhs=bout_row[:, oh * 512:(oh + 1) * 512], start=True, stop=True),
            waits=[borw, ident.ready, tu.evs[0], bout_rep_rdy])
        bout_rep_rdy = p.op("scalar", lambda sc, oh=oh: sc.copy(
            out=bout_rep[:, oh * 512:(oh + 1) * 512], in_=tu.banks[0][:]),
            waits=[bmm])
    tu.evs[0] = bout_rep_rdy
    Wxp_sb = sb(nc, es, "Wxp_s", [32, 1024], F32)
    Wxp16 = sb(nc, es, "Wxp16", [32, 1024], BF16)
    WxpT = sb(nc, es, "WxpT", [128, 8 * 64], BF16)
    tmp32 = sb(nc, es, "wxp_tmp", [128, 32], BF16)
    lw = p.dma("sync", Wxp_sb[:], Wxp_hbm)
    cwx = p.op("vector", lambda v: v.tensor_copy(out=Wxp16[:], in_=Wxp_sb[:]),
               waits=[lw])
    zwx = p.op("vector", lambda v: v.memset(WxpT[:], 0))
    wxp_rdy = None
    for dc in range(8):
        ev = tu.t(Wxp16[:, dc * 128:(dc + 1) * 128], tmp32[:], bf16=True,
                  waits=[cwx, zwx, wxp_rdy], cols=32)
        p.op("vector", lambda v, dc=dc: v.tensor_copy(
            out=WxpT[:, dc * 64:dc * 64 + 16], in_=tmp32[:, 0:16]), waits=[ev])
        wxp_rdy = p.op("vector", lambda v, dc=dc: v.tensor_copy(
            out=WxpT[:, dc * 64 + 32:dc * 64 + 48], in_=tmp32[:, 16:32]))
    bxp_pad = sb(nc, es, "bxp_pad", [64, 1], F32)
    bxz = p.op("vector", lambda v: v.memset(bxp_pad[:], 0.0))
    p.dma("sync", bxp_pad[0:16, :], bxp_hbm[0:16].unsqueeze(1), waits=[bxz])
    bxp_rdy = p.dma("sync", bxp_pad[32:48, :], bxp_hbm[16:32].unsqueeze(1))
    Worow = [sb(nc, es, f"worow{i}", [128, 1024], F32) for i in range(2)]
    Worow16 = [sb(nc, es, f"worow16{i}", [128, 1024], BF16) for i in range(2)]
    WoT = sb(nc, es, "WoT", [128, 8 * 1024], BF16)
    cast_done = [None, None]
    tp_done = [None, None]
    for oc in range(8):
        wb, wb16 = Worow[oc % 2], Worow16[oc % 2]
        lww = p.dma("sync", wb[:], Wout_hbm[oc * 128:(oc + 1) * 128, :],
                    waits=[cast_done[oc % 2]])
        cw = p.op("vector", lambda v, wb=wb, wb16=wb16: v.tensor_copy(
            out=wb16[:], in_=wb[:]), waits=[lww, tp_done[oc % 2]])
        cast_done[oc % 2] = cw
        for dc in range(8):
            tu.t(wb16[:, dc * 128:(dc + 1) * 128],
                 WoT[:, dc * 1024 + oc * 128: dc * 1024 + (oc + 1) * 128],
                 bf16=True, waits=[cw])
        tp_done[oc % 2] = p.milestone("tensor")
    wo_rdy = tu.done()

    xt = [sb(nc, es, f"l1xt{i}", [128, 1024], F32) for i in range(2)]
    bn6 = sb(nc, es, "l1bn6", [128, 12], F32)
    mv = sb(nc, es, "l1mv", [128, 2], F32)
    rstd = sb(nc, es, "l1rstd", [128, 1], F32)
    nmr = sb(nc, es, "l1nmr", [128, 1], F32)
    xn = sb(nc, es, "l1xn", [128, 1024], F32)
    xn16 = [sb(nc, es, f"l1xn16_{i}", [128, 1024], BF16) for i in range(2)]
    x0cols = sb(nc, es, "x0cols", [128, 32], F32)
    x0T = sb(nc, es, "x0T", [32, 128], F32)
    xn1T_stage = sb(nc, es, "xn1T_stage", [128, 1024], BF16)
    xsl = [sb(nc, es, f"xsl{i}", [128, 512], BF16) for i in range(2)]
    x0rep = [sb(nc, es, f"x0rep{i}", [128, 512], F32) for i in range(2)]
    BCT = sb(nc, es, "BCT", [64, 4096], F32)

    dtT = sb(nc, es, "dtT", [128, 8 * 512], F32)
    uT = sb(nc, es, "uT", [128, 8 * 512], F32)
    zt = sb(nc, es, "zt", [128, 512], F32)
    azt = sb(nc, es, "azt", [128, 512], F32)
    relz = zt
    Brep = [sb(nc, es, f"Brep{i}", [128, 512], F32) for i in range(2)]
    Crep = [sb(nc, es, f"Crep{i}", [128, 512], F32) for i in range(2)]
    dA8 = sb(nc, es, "dA8", [128, 8 * 512], F32)
    W8 = sb(nc, es, "W8", [128, 8 * 512], F32)
    state8 = sb(nc, es, "state8", [128, 8 * 512], F32)
    tmpy8 = W8
    carry = sb(nc, es, "carry", [128, 8 * 16], F32)
    yT = sb(nc, es, "yT", [128, 8 * 512], F32)
    yT16 = sb(nc, es, "yT16", [128, 8 * 512], BF16)
    sTt = sb(nc, es, "sTt", [128, 1024], F32)

    x0rep_last = {0: None, 1: None}
    for bi in range(n_batch):
        ln_done = {}; copy_done = {}; xnb_tp = {}; stage_dma = {}
        prev_batch_guard = p.milestone("tensor")
        for k in range(32):
            row0 = k * 128
            xbuf, xnb = xt[k % 2], xn16[k % 2]
            lwx = p.dma("sync", xbuf[:], x1_hbm[bi, row0:row0 + 128, :],
                        waits=[ln_done.get(k - 2)])
            p.op("vector", lambda v, xbuf=xbuf: v.bn_stats(
                out=bn6[:, 0:6], in_=xbuf[:, 0:512]), waits=[lwx])
            p.op("vector", lambda v, xbuf=xbuf: v.bn_stats(
                out=bn6[:, 6:12], in_=xbuf[:, 512:1024]))
            agg = p.op("vector", lambda v: v.bn_aggr(out=mv[:], in_=bn6[:]))
            p.op("scalar", lambda sc: sc.activation(
                out=rstd[:], in_=mv[:, 1:2], func=AF.Ln, bias=consts.col(1e-5)),
                waits=[agg, consts.ready])
            ex1 = p.op("scalar", lambda sc: sc.activation(
                out=rstd[:], in_=rstd[:], func=AF.Exp, scale=-0.5))
            p.op("vector", lambda v: v.tensor_scalar(
                out=nmr[:], in0=mv[:, 0:1], scalar1=rstd[:], scalar2=-1.0,
                op0=ALU.mult, op1=ALU.mult), waits=[ex1])
            lnw = p.op("vector", lambda v, xbuf=xbuf: v.tensor_scalar(
                out=xn[:], in0=xbuf[:], scalar1=rstd[:], scalar2=nmr[:],
                op0=ALU.mult, op1=ALU.add), waits=[copy_done.get(k - 1)])
            ln_done[k] = lnw
            x0c_done = p.op("vector", lambda v, k=k: v.tensor_copy(
                out=x0cols[:, k:k + 1], in_=xn[:, 0:1]))
            cpw = p.op("scalar", lambda sc, xnb=xnb: sc.copy(
                out=xnb[:], in_=xn[:]), waits=[lnw, xnb_tp.get(k - 2)])
            copy_done[k] = cpw
            for dc in range(8):
                tu.t(xnb[:, dc * 128:(dc + 1) * 128],
                     xn1T_stage[:, dc * 128:(dc + 1) * 128],
                     bf16=True,
                     waits=[cpw, prev_batch_guard, stage_dma.get(k - 1)])
            xnb_tp[k] = p.milestone("tensor")
            std = tu.done()
            for dc in range(8):
                sd = p.dma("sync",
                           xn1T_hbm[bi, dc * 128:(dc + 1) * 128, row0:row0 + 128],
                           xn1T_stage[:, dc * 128:(dc + 1) * 128], waits=[std])
            stage_dma[k] = sd
        xnt_done = stage_dma[31]
        x0ev = tu.t(x0cols[:], x0T[:], bf16=False, waits=[x0c_done], rows=32,
                    cols=128)
        x0w = p.dma("sync", x0_hbm[bi].rearrange("(a b) -> a b", b=128), x0T[:],
                    waits=[x0ev])
        x0r = {}
        x0r[0] = p.dma("sync", x0rep[0][:],
                       x0_hbm[bi, 0:512].unsqueeze(0).broadcast_to([128, 512]),
                       waits=[x0w, x0rep_last[0]])
        x0rep_used = dict(x0rep_last)
        bc_done = None
        xsl_last = {0: None, 1: None}
        for lc in range(8):
            last = None
            for dc in range(8):
                i2 = lc * 8 + dc
                xb = xsl[i2 % 2]
                lwbc = p.dma("sync", xb[:],
                             xn1T_hbm[bi, dc * 128:(dc + 1) * 128,
                                      lc * 512:(lc + 1) * 512],
                             waits=[xnt_done, xsl_last[i2 % 2]])
                last = p.op("tensor", lambda t, xb=xb, dc=dc: t.matmul(
                    out=pBC[0:64, :],
                    lhsT=WxpT[:, dc * 64:(dc + 1) * 64],
                    rhs=xb[:],
                    start=(dc == 0), stop=(dc == 7)),
                    waits=([wxp_rdy, bc_done, lwbc] if dc == 0 else [lwbc]))
                xsl_last[i2 % 2] = last
            bc_done = p.op("scalar", lambda sc, lc=lc: sc.activation(
                out=BCT[:, lc * 512:(lc + 1) * 512], in_=pBC[0:64, :],
                func=AF.Identity, bias=bxp_pad[:]),
                waits=[last, bxp_rdy])
        bct_w = p.dma("sync", bct_hbm[bi], BCT[:], waits=[bc_done])
        if dbg is not None:
            p.dma("sync", dbg["bct"][bi], BCT[:], waits=[bc_done])

        cz = p.op("vector", lambda v: v.memset(carry[:], 0.0))
        rep_last = {0: None, 1: None}
        da_last = {}
        tts_n = {}
        sp_evac = {}
        for c in range(n_chunk):
            l0 = c * 512
            if c + 1 < n_chunk:
                x0r[c + 1] = p.dma(
                    "sync", x0rep[(c + 1) % 2][:],
                    x0_hbm[bi, (c + 1) * 512:(c + 2) * 512].unsqueeze(0)
                    .broadcast_to([128, 512]),
                    waits=[x0w, x0rep_used.get((c + 1) % 2)])
            for dg in range(8):
                p.op("vector", lambda v, dg=dg, c=c: v.tensor_scalar(
                    out=zt[:], in0=x0rep[c % 2][:],
                    scalar1=wdt_col[:, dg:dg + 1], scalar2=bdt_col[:, dg:dg + 1],
                    op0=ALU.mult, op1=ALU.add),
                    waits=([x0r[c], wdt_rdy, bdt_rdy, da_last.get(7)]
                           if dg == 0 else []))
                abw = p.op("scalar", lambda sc: sc.activation(
                    out=azt[:], in_=zt[:], func=AF.Abs),
                    waits=[p.milestone("vector")])
                p.op("scalar", lambda sc: sc.activation(
                    out=azt[:], in_=azt[:], func=AF.Exp, scale=-1.0))
                lnp = p.op("scalar", lambda sc: sc.activation(
                    out=azt[:], in_=azt[:], func=AF.Ln, bias=consts.col(1.0)))
                p.op("vector", lambda v: v.tensor_scalar(
                    out=relz[:], in0=zt[:], scalar1=0.0, scalar2=None,
                    op0=ALU.max), waits=[abw])
                dtw = p.op("vector", lambda v, dg=dg: v.tensor_tensor(
                    out=dtT[:, dg * 512:(dg + 1) * 512], in0=relz[:], in1=azt[:],
                    op=ALU.add), waits=[lnp])
                xb = xsl[dg % 2]
                lwu = p.dma("sync", xb[:],
                            xn1T_hbm[bi, dg * 128:(dg + 1) * 128, l0:l0 + 512],
                            waits=[xnt_done, xsl_last[dg % 2]])
                if dbg is not None and "xsl" in dbg:
                    p.dma("sync", dbg["xsl"][bi, dg * 128:(dg + 1) * 128,
                                             l0:l0 + 512], xb[:], waits=[lwu])
                xsl_last[dg % 2] = p.op("vector",
                                   lambda v, dg=dg, xb=xb: v.tensor_tensor(
                    out=uT[:, dg * 512:(dg + 1) * 512],
                    in0=dtT[:, dg * 512:(dg + 1) * 512],
                    in1=xb[:], op=ALU.mult), waits=[lwu])
            x0rep_used[c % 2] = p.milestone("vector")
            x0rep_last[c % 2] = x0rep_used[c % 2]
            dtu_done = p.milestone("vector")
            for n in range(16):
                bb, cb = Brep[n % 2], Crep[n % 2]
                bw = p.dma("sync", bb[:],
                           bct_hbm[bi, n, l0:l0 + 512].unsqueeze(0)
                           .broadcast_to([128, 512]),
                           waits=[bct_w, rep_last[n % 2]])
                cw2 = p.dma("sync", cb[:],
                            bct_hbm[bi, 32 + n, l0:l0 + 512].unsqueeze(0)
                            .broadcast_to([128, 512]),
                            waits=[bct_w, rep_last[n % 2]])
                bbb = bb[:].unsqueeze(1).broadcast_to([128, 8, 512])
                cbb = cb[:].unsqueeze(1).broadcast_to([128, 8, 512])
                p.op("vector", lambda v, bbb=bbb: v.tensor_tensor(
                    out=W8[:].rearrange("p (g l) -> p g l", l=512),
                    in0=uT[:].rearrange("p (g l) -> p g l", l=512),
                    in1=bbb, op=ALU.mult), waits=[bw, dtu_done])
                for dg in range(8):
                    j = n * 8 + dg
                    da_last[dg] = p.op("scalar", lambda sc, dg=dg, n=n:
                                       sc.activation(
                        out=dA8[:, dg * 512:(dg + 1) * 512],
                        in_=dtT[:, dg * 512:(dg + 1) * 512],
                        func=AF.Exp, scale=A_sb[:, dg * 16 + n:dg * 16 + n + 1]),
                        waits=[dtu_done, A_rdy, tts_n.get(n - 1)])
                daw = da_last[7]
                for dg in range(8):
                    p.op("vector", lambda v, dg=dg, n=n: v.tensor_tensor_scan(
                        out=state8[:, dg * 512:(dg + 1) * 512],
                        data0=dA8[:, dg * 512:(dg + 1) * 512],
                        data1=W8[:, dg * 512:(dg + 1) * 512],
                        initial=carry[:, dg * 16 + n:dg * 16 + n + 1],
                        op0=ALU.mult, op1=ALU.add),
                        waits=[daw] if dg == 0 else [])
                tts_n[n] = p.milestone("vector")
                p.op("vector", lambda v, n=n: v.tensor_copy(
                    out=carry[:].rearrange("p (g n_) -> p g n_", n_=16)[:, :, n],
                    in_=state8[:].rearrange("p (g l) -> p g l", l=512)[:, :, 511]))
                if n == 0:
                    p.op("vector", lambda v, cbb=cbb: v.tensor_tensor(
                        out=yT[:].rearrange("p (g l) -> p g l", l=512),
                        in0=state8[:].rearrange("p (g l) -> p g l", l=512),
                        in1=cbb, op=ALU.mult),
                        waits=[cw2, sp_evac.get((c - 1, 3))])
                else:
                    p.op("vector", lambda v, cbb=cbb: v.tensor_tensor(
                        out=tmpy8[:].rearrange("p (g l) -> p g l", l=512),
                        in0=state8[:].rearrange("p (g l) -> p g l", l=512),
                        in1=cbb, op=ALU.mult), waits=[cw2])
                    p.op("vector", lambda v: v.tensor_tensor(
                        out=yT[:], in0=yT[:], in1=tmpy8[:], op=ALU.add))
                rep_last[n % 2] = p.milestone("vector")
            ycast = None
            mmguard = p.milestone("tensor")
            for dg in range(8):
                ycast = p.op("vector", lambda v, dg=dg: v.tensor_copy(
                    out=yT16[:, dg * 512:(dg + 1) * 512],
                    in_=yT[:, dg * 512:(dg + 1) * 512]),
                    waits=[mmguard] if dg == 0 else [])
            for ts4 in range(4):
                pm = pSP[ts4 % 2]
                last = None
                for dg in range(8):
                    for oh in range(2):
                        last = p.op("tensor", lambda t, dg=dg, oh=oh, ts4=ts4,
                                    pm=pm: t.matmul(
                            out=pm[oh][:],
                            lhsT=yT16[:, dg * 512 + ts4 * 128: dg * 512 + (ts4 + 1) * 128],
                            rhs=WoT[:, dg * 1024 + oh * 512: dg * 1024 + (oh + 1) * 512],
                            start=(dg == 0), stop=(dg == 7)),
                            waits=([ycast, wo_rdy, sp_evac.get((c, ts4 - 2)),
                                    sp_evac.get((c - 1, ts4 + 2))]
                                   if (dg == 0 and oh == 0) else []))
                ev = None
                for oh in range(2):
                    ev = p.op("vector", lambda v, oh=oh, pm=pm: v.tensor_tensor(
                        out=sTt[:, oh * 512:(oh + 1) * 512], in0=pm[oh][:],
                        in1=bout_rep[:, oh * 512:(oh + 1) * 512], op=ALU.add),
                        waits=([last, bout_rep_rdy,
                                sp_evac.get((c, ts4 - 1)) or sp_evac.get((c - 1, 3))]
                               if oh == 0 else []))
                sp_evac[(c, ts4)] = p.dma(
                    "sync", s_hbm[bi, l0 + ts4 * 128: l0 + (ts4 + 1) * 128, :],
                    sTt[:], waits=[ev])
    return



def emit_L1B(p, nc, es, x1_hbm, xn1T_hbm, s_hbm, rm_hbm, rup_hbm, wc_hbm,
             out_hbm, wc16_hbm, stats_hbm, gpart_hbm, ident, tu, n_batch=2,
             n_tc=16):
    pG = [psbank(nc, es, "l1b_g0"), psbank(nc, es, "l1b_g1")]
    pC = [psbank(nc, es, "l1b_c0"), psbank(nc, es, "l1b_c1"),
          psbank(nc, es, "l1b_c2"), psbank(nc, es, "l1b_c3")]

    def dma_t(out, in_, waits=()):
        s = p._dmasem(out)
        wl = [(w[0], w[1]) for w in p._flat(waits)]
        wl.append((p.junk, p.junk.n))
        s.n += 16
        p.ops["sync"].append(
            (lambda e, o=out, i=in_: e.dma_start(out=o, in_=i, transpose=True),
             wl, s, 16))
        p.ops["sync"].append((lambda e: e.nop(), [], p.junk, 1))
        return (s, s.n)

    stats_sb = sb(nc, es, "stats_sb", [128, 128 * 128], BF16)
    es2 = ExitStack()
    wcl = [sb(nc, es2, f"wcl{i}", [128, 4096], F32) for i in range(2)]
    wcl16 = [sb(nc, es2, f"wcl16{i}", [128, 4096], BF16) for i in range(2)]
    cast_prev = [None, None]
    st_prev = [None, None]
    for oc in range(8):
        for kc in range(4):
            i2 = oc * 4 + kc
            wb, wb16 = wcl[i2 % 2], wcl16[i2 % 2]
            lw = p.dma("sync", wb[:],
                       wc_hbm[oc * 128:(oc + 1) * 128, kc * 4096:(kc + 1) * 4096],
                       waits=[cast_prev[i2 % 2]])
            cw = p.op("vector", lambda v, wb=wb, wb16=wb16: v.tensor_copy(
                out=wb16[:], in_=wb[:]), waits=[lw, st_prev[i2 % 2]])
            cast_prev[i2 % 2] = cw
            st_prev[i2 % 2] = p.dma(
                "sync", wc16_hbm[oc * 128:(oc + 1) * 128, kc * 4096:(kc + 1) * 4096],
                wb16[:], waits=[cw])
    wc16_done = p.dma_barrier()
    zt16 = sb(nc, es2, "zt16", [128, 2048], BF16)
    zw = p.op("vector", lambda v: v.memset(zt16[:], 0))
    stz = None
    for i in range(8):
        stz = p.dma("sync", stats_hbm[:, i * 2048:(i + 1) * 2048], zt16[:],
                    waits=[zw])
    RMT = sb(nc, es2, "RMT", [16, 8 * 128], BF16)
    RUPT = sb(nc, es2, "RUPT", [16, 32 * 128], BF16)
    rmld = sb(nc, es2, "rmld", [128, 16], F32)
    rmld16 = sb(nc, es2, "rmld16", [128, 16], BF16)
    gtmp = sb(nc, es2, "gtmp", [16, 64], BF16)
    g_st = {}
    for h in range(4):
        rmt_done = None
        for g8 in range(8):
            lw = p.dma("sync", rmld[:],
                       rm_hbm[h, g8 * 8:(g8 + 1) * 8].rearrange("r i j -> (r i) j"),
                       waits=[rmt_done])
            cw = p.op("vector", lambda v: v.tensor_copy(out=rmld16[:], in_=rmld[:]),
                      waits=[lw])
            rmt_done = tu.t(rmld16[:], RMT[:, g8 * 128:(g8 + 1) * 128], bf16=True,
                            waits=[cw], rows=16, cols=128)
        rupt_done = rmt_done
        for g2 in range(32):
            lw = p.dma("sync", rmld[:],
                       rup_hbm[h, g2 * 2:(g2 + 1) * 2].rearrange("r k j -> (r k) j"),
                       waits=[rupt_done])
            cw = p.op("vector", lambda v: v.tensor_copy(out=rmld16[:], in_=rmld[:]),
                      waits=[lw])
            rupt_done = tu.t(rmld16[:], RUPT[:, g2 * 128:(g2 + 1) * 128], bf16=True,
                             waits=[cw], rows=16, cols=128)
        for r in range(64):
            pgb = pG[r % 2]
            mm = p.op("tensor", lambda t, r=r, pgb=pgb: t.matmul(
                out=pgb[0:16, 0:64],
                lhsT=RMT[:, (r // 8) * 128 + (r % 8) * 16:(r // 8) * 128 + (r % 8) * 16 + 16],
                rhs=RUPT[:, (r // 2) * 128 + (r % 2) * 64:(r // 2) * 128 + (r % 2) * 64 + 64],
                start=True, stop=True),
                waits=[rupt_done, g_st.get((h, r - 2))])
            ev = p.op("scalar", lambda sc, pgb=pgb: sc.copy(out=gtmp[:],
                                                            in_=pgb[0:16, 0:64]),
                      waits=[mm, g_st.get((h, r - 1))])
            pi = h * 32 + r // 2
            lr = 2 * ((r // 2) % 4) + (r % 2)
            col0 = pi * 128 + (r % 2) * 64
            g_st[(h, r)] = p.dma(
                "sync", stats_hbm[lr * 16:(lr + 1) * 16, col0:col0 + 64],
                gtmp[:], waits=[ev, stz])
    gst_done = g_st[(3, 63)]

    stats_rdy = None
    for i in range(8):
        stats_rdy = p.dma("sync", stats_sb[:, i * 2048:(i + 1) * 2048],
                          stats_hbm[:, i * 2048:(i + 1) * 2048],
                          waits=[gst_done])

    es2.close()
    wcA = sb(nc, es, "wcA", [128, 64 * 1024], BF16)
    xsl2 = [sb(nc, es, f"xsl2_{i}", [128, 8 * 256], BF16) for i in range(2)]
    hT = [sb(nc, es, f"hT{i}", [128, 256], BF16) for i in range(2)]
    gst2 = [sb(nc, es, f"gst2_{i}", [128, 1024], F32) for i in range(2)]
    gab = [sb(nc, es, f"gab{i}", [128, 1024], F32) for i in range(2)]
    ssl = [sb(nc, es, f"ssl{i}", [128, 1024], F32) for i in range(2)]
    x1sl = [sb(nc, es, f"x1sl{i}", [128, 1024], F32) for i in range(2)]

    xsl2_last = {0: None, 1: None}
    gab_last = {0: None, 1: None}
    ssl_last = {0: None, 1: None}
    x1_last = {0: None, 1: None}
    gst_last = {0: None, 1: None}
    zps_last = {0: None, 1: None}
    ht_last = {0: None, 1: None}
    ot_st = {}
    gstage_st = None
    wca_last = None
    gT_free = None
    for ph in range(2):
        wca_done = None
        for pl in range(64):
            pg = ph * 64 + pl
            wca_done = dma_t(wcA[:, pl * 1024:(pl + 1) * 1024],
                             wc16_hbm[0:1024, pg * 128:(pg + 1) * 128],
                             waits=([wc16_done, gst_done, stats_rdy, wca_last]
                                    if pl == 0 else []))
        gpw = p.dma_barrier() if ph == 1 else None
        for bi in range(n_batch):
            for tc in range(n_tc):
                t0 = tc * 256
                xb = xsl2[tc % 2]
                xw = None
                for dg in range(8):
                    xw = p.dma("sync", xb[:, dg * 256:(dg + 1) * 256],
                               xn1T_hbm[bi, dg * 128:(dg + 1) * 128, t0:t0 + 256],
                               waits=[xsl2_last[tc % 2]] if dg == 0 else [])
                mm_last = None
                gws = {}
                def emit_rails(pl):
                    pg = ph * 64 + pl
                    dgb = (pg % 32) // 4
                    zb = pG[pl % 2]
                    mm = p.op("tensor", lambda t, pg=pg, dgb=dgb, zb=zb,
                              xb=xb: t.matmul(
                        out=zb[:, 0:256],
                        lhsT=stats_sb[:, pg * 128:(pg + 1) * 128],
                        rhs=xb[:, dgb * 256:(dgb + 1) * 256],
                        start=True, stop=True),
                        waits=([stats_rdy, wca_done, xw, gT_free,
                                zps_last[pl % 2]] if pl < 2
                               else [zps_last[pl % 2]]))
                    hb = hT[pl % 2]
                    gw = p.op("scalar", lambda sc, zb=zb, hb=hb: sc.activation(
                        out=hb[:], in_=zb[:, 0:256], func=AF.Gelu),
                        waits=[mm, ht_last[pl % 2]])
                    zps_last[pl % 2] = gw
                    gws[pl] = gw
                def emit_cproj(pl):
                    hb = hT[pl % 2]
                    last = None
                    for ts2 in range(2):
                        for oh in range(2):
                            last = p.op("tensor", lambda t, pl=pl, ts2=ts2,
                                        oh=oh, hb=hb: t.matmul(
                                out=pC[ts2 * 2 + oh][:],
                                lhsT=hb[:, ts2 * 128:(ts2 + 1) * 128],
                                rhs=wcA[:, pl * 1024 + oh * 512:
                                        pl * 1024 + (oh + 1) * 512],
                                start=(pl == 0), stop=(pl == 63)),
                                waits=([gws[pl]] if (ts2 == 0 and oh == 0) else []))
                    ht_last[pl % 2] = last
                    return last
                emit_rails(0)
                for pl in range(64):
                    if pl + 1 < 64:
                        emit_rails(pl + 1)
                    mm_last = emit_cproj(pl)
                allmm = mm_last
                xsl2_last[tc % 2] = allmm
                if ph == 0:
                    for ts2 in range(2):
                        gs = gst2[ts2]
                        ev = None
                        for oh in range(2):
                            ev = p.op("scalar", lambda sc, ts2=ts2, oh=oh,
                                      gs=gs: sc.copy(
                                out=gs[:, oh * 512:(oh + 1) * 512],
                                in_=pC[ts2 * 2 + oh][:]),
                                waits=([allmm, gst_last[ts2]] if oh == 0 else []))
                        gst_last[ts2] = p.dma(
                            "sync",
                            gpart_hbm[bi, t0 + ts2 * 128:t0 + (ts2 + 1) * 128, :],
                            gs[:], waits=[ev])
                    gT_free = ev
                else:
                    silw = None
                    for ts2 in range(2):
                        gb = gab[ts2]
                        gld = p.dma("sync", gb[:],
                                    gpart_hbm[bi, t0 + ts2 * 128:t0 + (ts2 + 1) * 128, :],
                                    waits=[gab_last[ts2], gpw])
                        addw = None
                        for oh in range(2):
                            addw = p.op("vector", lambda v, ts2=ts2, oh=oh,
                                        gb=gb: v.tensor_tensor(
                                out=gb[:, oh * 512:(oh + 1) * 512],
                                in0=pC[ts2 * 2 + oh][:],
                                in1=gb[:, oh * 512:(oh + 1) * 512], op=ALU.add),
                                waits=([allmm, gld] if oh == 0 else []))
                        silw = p.op("scalar", lambda sc, gb=gb: sc.activation(
                            out=gb[:], in_=gb[:], func=AF.Silu), waits=[addw])
                        gab_last[ts2] = silw
                    gT_free = silw
                    for ts2 in range(2):
                        gb = gab[ts2]
                        sl = ssl[ts2]
                        x1b = x1sl[ts2]
                        sld = p.dma("sync", sl[:],
                                    s_hbm[bi, t0 + ts2 * 128:t0 + (ts2 + 1) * 128, :],
                                    waits=[ssl_last[ts2]])
                        x1w = p.dma("sync", x1b[:],
                                    x1_hbm[bi, t0 + ts2 * 128:t0 + (ts2 + 1) * 128, :],
                                    waits=[x1_last[ts2]])
                        sgw = p.op("vector", lambda v, gb=gb, sl=sl: v.tensor_tensor(
                            out=sl[:], in0=sl[:], in1=gb[:], op=ALU.mult),
                            waits=[gab_last[ts2], sld])
                        gab_last[ts2] = sgw
                        ow = p.op("vector", lambda v, sl=sl, x1b=x1b: v.tensor_tensor(
                            out=x1b[:], in0=x1b[:], in1=sl[:], op=ALU.add),
                            waits=[sgw, x1w])
                        ssl_last[ts2] = ow
                        x1_last[ts2] = p.dma(
                            "sync", out_hbm[bi, t0 + ts2 * 128:t0 + (ts2 + 1) * 128, :],
                            x1b[:], waits=[ow])
        wca_last = p.milestone("tensor")
    return
import ml_dtypes
from concourse.bass_utils import run_bass_kernel_spmd

NB = 2


def _build_fused():
    nc = bass.Bass("TRN2", target_bir_lowering=False, debug=False)
    ap = lambda name, shape, dt, kind: nc.dram_tensor(name, shape, dt, kind=kind).ap()
    x = ap("x", [NB, 4096, 1024], F32, "ExternalInput")
    W = ap("W", [1024, 1024], F32, "ExternalInput")
    b = ap("b", [1024], F32, "ExternalInput")
    fw = ap("fw", [1024, 1024], F32, "ExternalInput")
    fb = ap("fb", [1024], F32, "ExternalInput")
    al = ap("Alog", [1024, 16], F32, "ExternalInput")
    wdt = ap("Wdt", [1024, 1], F32, "ExternalInput")
    bdt = ap("bdt", [1024], F32, "ExternalInput")
    wxp = ap("Wxp", [32, 1024], F32, "ExternalInput")
    bxp = ap("bxp", [32], F32, "ExternalInput")
    wo = ap("Wout", [1024, 1024], F32, "ExternalInput")
    bo = ap("bout", [1024], F32, "ExternalInput")
    rm = ap("rm", [4, 64, 16, 16], F32, "ExternalInput")
    rup = ap("rup", [4, 64, 64, 16], F32, "ExternalInput")
    wc = ap("wc", [1024, 16384], F32, "ExternalInput")
    out = ap("out", [NB, 4096, 1024], F32, "ExternalOutput")
    fT = ap("fT", [NB, 1024, 4096], BF16, "Internal")
    f = ap("f", [NB, 4096, 1024], BF16, "Internal")
    xd = ap("xd", [NB, 4096, 1024], BF16, "Internal")
    xs = ap("xs", [NB, 4096, 32], F32, "Internal")
    x1 = ap("x1", [NB, 4096, 1024], F32, "Internal")
    xn1T = ap("xn1T", [NB, 1024, 4096], BF16, "Internal")
    sT = ap("sT", [NB, 4096, 1024], F32, "Internal")
    x0s = ap("x0s", [NB, 4096], F32, "Internal")
    bcts = ap("bcts", [NB, 64, 4096], F32, "Internal")
    wc16 = ap("wc16s", [1024, 16384], BF16, "Internal")
    stats = ap("statss", [128, 16384], BF16, "Internal")
    gp = ap("gparts", [NB, 4096, 1024], F32, "Internal")
    es = ExitStack()
    with es, nc.Block() as block:
        p = Prog(nc, es)
        ident = Ident(p, nc, es)
        tu = TUnit(p, nc, es, ident)
        consts = Consts(p, nc, es, [1e-5, 1.0])
        with ExitStack() as es1:
            emit_L0A(p, nc, es1, x, W, b, fT, f, xd, xs, ident, tu, consts,
                     n_batch=NB)
            p.fence()
        with ExitStack() as es2:
            emit_L0B(p, nc, es2, x, fT, f, xd, fw, fb, x1, ident, tu,
                     n_batch=NB)
            p.fence()
        with ExitStack() as es3:
            emit_L1A(p, nc, es3, x1, al, wdt, bdt, wxp, bxp, wo, bo, xn1T, sT,
                     x0s, bcts, ident, tu, consts, n_batch=NB)
            p.fence()
        with ExitStack() as es4:
            emit_L1B(p, nc, es4, x1, xn1T, sT, rm, rup, wc, out, wc16, stats,
                     gp, ident, tu, n_batch=NB)
            p.fence()
        p.emit(block)
    return nc


def kernel(L0_attn_x, L0_attn_W, L0_attn_b, L0_ffn_weight, L0_ffn_bias,
           L1_attn_A_log, L1_attn_W_dt, L1_attn_b_dt, L1_attn_W_xp,
           L1_attn_b_xp, L1_attn_W_out, L1_attn_b_out, L1_ffn_rail_maker,
           L1_ffn_rail_up_proj, L1_ffn_w_cproj):
    cores = list(range(8))
    nc = _build_fused()
    maps = [{"x": np.ascontiguousarray(L0_attn_x[2 * c:2 * c + 2]),
             "W": L0_attn_W, "b": L0_attn_b, "fw": L0_ffn_weight,
             "fb": L0_ffn_bias, "Alog": L1_attn_A_log, "Wdt": L1_attn_W_dt,
             "bdt": L1_attn_b_dt, "Wxp": L1_attn_W_xp, "bxp": L1_attn_b_xp,
             "Wout": L1_attn_W_out, "bout": L1_attn_b_out,
             "rm": L1_ffn_rail_maker, "rup": L1_ffn_rail_up_proj,
             "wc": L1_ffn_w_cproj} for c in cores]
    res = run_bass_kernel_spmd(nc, maps, core_ids=cores).results
    return np.concatenate([np.asarray(res[c]["out"], dtype=np.float32)
                           for c in cores], axis=0)
